# version 13
# baseline (speedup 1.0000x reference)
import contextlib
import numpy as np
import ml_dtypes
import concourse.bass as bass
import concourse.mybir as mybir
from concourse.bass_utils import run_bass_kernel_spmd

F32 = mybir.dt.float32
BF16 = mybir.dt.bfloat16
I8 = mybir.dt.int8
U32 = mybir.dt.uint32
AF = mybir.ActivationFunctionType
ALU = mybir.AluOpType

D = 2048
T = 2048
BATCH = 4
TC = 512
NCH = T // TC
FT = 256
DBA = 16
DBM = 8
S = 16
RK = 128
NIT = 64
HG = 16
INV_D = 1.0 / D
EPS_LN = 1e-5
PAIRS = [[0, 1], [2, 3], [4, 5], [6, 7]]
PACE = 26.0

ENGINES = ("sync", "pe", "act", "dve", "pool")
OWN_SEM = {"pe": "Pe", "act": "Ac", "dve": "Dv", "pool": "Po"}
def is_dma_sem(s):
    return s[0] in ("L", "G")


class Prog:

    def __init__(self):
        self.ops = []
        self.cnt = {}

    def ev_last(self, sem):
        return (sem, self.cnt.get(sem, 0))

    def add(self, eng, fn, waits=(), inc=None):
        ev = None
        if inc is not None:
            self.cnt[inc] = self.cnt.get(inc, 0) + 1
            ev = (inc, self.cnt[inc])
        self.ops.append((eng, fn, tuple(w for w in waits if w is not None), inc))
        return ev

    def emit_engine(self, name, eng, sems):
        known = {}
        own = OWN_SEM.get(name)
        for oeng, fn, waits, inc in self.ops:
            if oeng != name:
                continue
            for sem, val in waits:
                if sem.endswith("!"):
                    sem = sem[:-1]
                elif sem == own:
                    continue
                tgt = val * 16 if is_dma_sem(sem) else val
                if known.get(sem, 0) < tgt:
                    eng.wait_ge(sems[sem], tgt)
                    known[sem] = tgt
            ins = fn(eng)
            if inc is not None:
                ins.then_inc(sems[inc], 16 if is_dma_sem(inc) else 1)


PART = "all"


def build_nc():
    nc = bass.Bass("TRN2", target_bir_lowering=False, debug=False)

    xT = nc.dram_tensor("xT", [D, T + 3], F32, kind="ExternalInput").ap()
    cw = nc.dram_tensor("cw", [128, 64], F32, kind="ExternalInput").ap()
    cbi = nc.dram_tensor("cbi", [128, 16], F32, kind="ExternalInput").ap()
    dpi = nc.dram_tensor("dpi", [128, 8], F32, kind="ExternalInput").ap()
    dtbi = nc.dram_tensor("dtbi", [128, 8], F32, kind="ExternalInput").ap()
    alog = nc.dram_tensor("alog", [128, 128], F32, kind="ExternalInput").ap()
    wxp = nc.dram_tensor("wxp", [D, 160], F32, kind="ExternalInput").ap()
    wdt = nc.dram_tensor("wdt", [RK, 1024], F32, kind="ExternalInput").ap()
    onescol = nc.dram_tensor("onescol", [1, 128], F32, kind="ExternalInput").ap()
    ones128 = nc.dram_tensor("ones128", [128, 1], F32, kind="ExternalInput").ap()
    hgu = nc.dram_tensor("hgu", [NIT, 128, 4096], I8, kind="ExternalInput").ap()
    hd = nc.dram_tensor("hd", [NIT, 128, 2048], I8, kind="ExternalInput").ap()
    sgd = nc.dram_tensor("sgd", [128, 64], F32, kind="ExternalInput").ap()
    sud = nc.dram_tensor("sud", [128, 64], F32, kind="ExternalInput").ap()
    sdd = nc.dram_tensor("sdd", [128, 16], F32, kind="ExternalInput").ap()
    offs = nc.dram_tensor("offs", [1, 4], U32, kind="ExternalInput").ap()

    out = nc.dram_tensor("out", [D, NCH * FT], F32, kind="ExternalOutput").ap()

    ybin = [nc.dram_tensor(f"ybin{t}", [1024, TC], F32) for t in range(NCH)]
    ybag = nc.dram_tensor("ybag", [NCH * D, TC], F32)
    bcd = [nc.dram_tensor(f"bcd{i}", [32, TC], F32) for i in range(2)]
    fuse_d = nc.dram_tensor("fuse_d", [NIT, 128, FT], BF16)
    hgu_bf = nc.dram_tensor("hgu_bf", [NIT, 128, 4096], BF16)
    hd_bf = nc.dram_tensor("hd_bf", [NIT, 128, 2048], BF16)

    ctx = contextlib.ExitStack()
    sb = lambda name, shape, dt: ctx.enter_context(nc.sbuf_tensor(name, shape, dt))
    psb = lambda name, shape: ctx.enter_context(nc.psum_tensor(name, shape, F32))

    with ctx:
        cw_s = sb("cw_s", [128, 64], F32)
        cb_s = sb("cb_s", [128, 16], F32)
        dp_s = sb("dp_s", [128, 8], F32)
        dtb_s = sb("dtb_s", [128, 8], F32)
        A_s = sb("A_s", [128, 128], F32)
        Atmp_s = sb("Atmp_s", [128, 128], F32)
        wxp_s = sb("wxp_s", [128, 16 * 160], BF16)
        wdt_s = sb("wdt_s", [128, 1024], BF16)
        onec_s = sb("onec_s", [1, 128], F32)
        one128_s = sb("one128_s", [128, 1], F32)
        sg_s = sb("sg_s", [128, 64], F32)
        su_s = sb("su_s", [128, 64], F32)
        sd_s = sb("sd_s", [128, 16], F32)
        offs_s = sb("offs_s", [1, 4], U32)
        state = sb("state", [128, 128], F32)
        dtr_s = sb("dtr_s", [128, TC], BF16)
        bc_s = sb("bc_s", [32, TC], F32)

        AB = 181 * 1024
        arena = sb("arena", [128, AB], I8)
        a32 = arena.bitcast(F32)
        a16 = arena.bitcast(BF16)

        class Alloc:
            def __init__(self):
                self.cur = 0

            def f32(self, nelem):
                off = (self.cur + 3) // 4 * 4
                self.cur = off + nelem * 4
                assert self.cur <= AB, f"arena overflow {self.cur}"
                return off // 4

            def bf16(self, nelem):
                off = (self.cur + 1) // 2 * 2
                self.cur = off + nelem * 2
                assert self.cur <= AB, f"arena overflow {self.cur}"
                return off // 2

        M = Alloc()
        o_xt = [M.f32(TC + 3) for _ in range(3)]
        o_acc = [M.f32(TC) for _ in range(2)]
        o_acc2 = [M.f32(TC) for _ in range(2)]
        o_xcm = M.f32(DBM * TC)
        o_xcb = M.bf16(DBA * TC)
        o_dt = M.f32(DBM * TC)
        o_w = M.f32(DBM * TC)
        o_y = M.f32(DBM * TC)
        o_dA = [M.f32(TC + 1) for _ in range(2)]
        o_u = [M.f32(TC) for _ in range(2)]
        o_dbx = [M.f32(TC + 1) for _ in range(2)]
        o_h = [M.f32(TC + 1) for _ in range(2)]
        o_bsb = [M.f32(TC) for _ in range(2)]
        o_csb = [M.f32(TC) for _ in range(2)]
        o_usb = M.f32(TC)
        o_rl = M.f32(TC)
        o_sp0 = M.f32(TC)
        o_sp1 = M.f32(TC)
        o_yf = M.f32(HG * FT)
        o_x1b = M.bf16(HG * FT)
        o_wgu = [M.bf16(4096) for _ in range(2)]
        o_wd = [M.bf16(512) for _ in range(2)]
        o_frd = [M.bf16(FT) for _ in range(2)]
        o_gt = [M.f32(FT) for _ in range(2)]
        o_ft = [M.bf16(FT) for _ in range(2)]
        o_sq = [M.f32(FT) for _ in range(2)]
        o_rows = M.f32(4 * FT)
        o_ost = [M.f32(FT) for _ in range(2)]

        f32v = lambda off, n: a32[0:128, off:off + n]
        b16v = lambda off, n: a16[0:128, off:off + n]

        xtv = lambda i: f32v(o_xt[i], TC + 3)
        xts = lambda i, a, n: f32v(o_xt[i] + a, n)
        accv = lambda i: f32v(o_acc[i], TC)
        acc2v = lambda i: f32v(o_acc2[i], TC)
        xcm = lambda db: f32v(o_xcm + db * TC, TC)
        xcb = lambda db: b16v(o_xcb + db * TC, TC)
        dtv = lambda db: f32v(o_dt + db * TC, TC)
        wv_ = lambda db: f32v(o_w + db * TC, TC)
        yv = lambda db: f32v(o_y + db * TC, TC)
        dAv = lambda i: f32v(o_dA[i], TC + 1)
        dAx = lambda i: f32v(o_dA[i] + 1, TC)
        uv = lambda i: f32v(o_u[i], TC)
        dbxv = lambda i: f32v(o_dbx[i], TC + 1)
        dbxx = lambda i: f32v(o_dbx[i] + 1, TC)
        dbx0 = lambda i: f32v(o_dbx[i], 1)
        hv = lambda i: f32v(o_h[i], TC + 1)
        hx = lambda i: f32v(o_h[i] + 1, TC)
        vv = dAx
        bsbv = lambda i: f32v(o_bsb[i], TC)
        csbv = lambda i: f32v(o_csb[i], TC)
        usbv = f32v(o_usb, TC)
        rlv = f32v(o_rl, TC)
        sp0v = f32v(o_sp0, TC)
        sp1v = f32v(o_sp1, TC)

        yfv = lambda hg: f32v(o_yf + hg * FT, FT)
        x1bv = lambda hg: b16v(o_x1b + hg * FT, FT)
        wguv = lambda i, a, n: b16v(o_wgu[i] + a, n)
        wdv = lambda i, a, n: b16v(o_wd[i] + a, n)
        frdv = lambda i: b16v(o_frd[i], FT)
        gtv = lambda i: f32v(o_gt[i], FT)
        ftv = lambda i: b16v(o_ft[i], FT)
        sqv = lambda i: f32v(o_sq[i], FT)
        row = lambda i: a32[0:1, o_rows + i * FT: o_rows + (i + 1) * FT]
        ostv = lambda i: f32v(o_ost[i], FT)

        pb = [psb(f"pb{i}", [128, 512]) for i in range(8)]

        sem_names = (["Lc", "Gc", "Lbc", "Cc", "Gf", "Gp", "Pe", "Ac", "Dv", "Po"]
                     + [f"Lb{i}" for i in range(4)]
                     + [f"Lx{i}" for i in range(3)] + [f"Ly{i}" for i in range(NCH)]
                     + [f"Gw{i}" for i in range(2)] + [f"Gv{i}" for i in range(2)]
                     + [f"Lr{i}" for i in range(2)] + [f"Lf{i}" for i in range(2)]
                     + [f"Lo{i}" for i in range(2)])
        sems = {n: ctx.enter_context(nc.semaphore(n)) for n in sem_names}

        P = Prog()

        const_loads = [("cw", cw_s, cw), ("cb", cb_s, cbi), ("dp", dp_s, dpi),
                       ("dtb", dtb_s, dtbi), ("alog", Atmp_s, alog),
                       ("onec", onec_s, onescol),
                       ("one128", one128_s, ones128), ("sg", sg_s, sgd),
                       ("su", su_s, sud), ("sd", sd_s, sdd), ("offs", offs_s, offs)]
        const_evs = {}
        for name, dst, src in const_loads:
            const_evs[name] = P.add("sync", lambda e, d=dst, s=src: e.dma_start(out=d[:], in_=s), inc="Lc")
        ev_wxp = P.add("pool", lambda e: e.dma_start(
            out=wxp_s.rearrange("p (kg f) -> p kg f", kg=16),
            in_=wxp.rearrange("(kg p) f -> p kg f", p=128)), inc="Gc")
        ev_wdt = P.add("pool", lambda e: e.dma_start(out=wdt_s[:], in_=wdt), inc="Gc")
        ev_flush = P.add("sync", lambda e: e.dma_start(out=Atmp_s[:], in_=alog), inc="Lc")
        evA1 = P.add("act", lambda e: e.activation(A_s[:], Atmp_s[:], AF.Exp),
                     waits=[ev_flush, const_evs["alog"]], inc="Ac")
        evA = P.add("dve", lambda e: e.tensor_scalar_mul(A_s[:], A_s[:], -1.0),
                    waits=[evA1], inc="Dv")
        ev_ms = []
        for i in range(2):
            ev_ms.append(P.add("dve", lambda e, i=i: e.memset(
                f32v(o_dA[i], 1), 0.0), inc="Dv"))
        ev_st0 = P.add("dve", lambda e: e.memset(state[:], 0.0), inc="Dv")
        for it in range(NIT):
            P.add("pool", lambda e, it=it: e.dma_start(
                out=hgu_bf[it], in_=hgu[it]), inc="Gp")
        for it in range(NIT):
            P.add("pool", lambda e, it=it: e.dma_start(
                out=hd_bf[it], in_=hd[it]), inc="Gp")

        def pump_casts(n=6):
            pass

        ag_evs = []
        sl_yf_war = [[]]
        sl_pb23_war = [[]]
        sl_pb67_war = [[]]
        sl_x1b_war = [[]]
        pbB_war = [None] * 4
        last_frd_ev = [[None] * NIT]

        def ffn_stream():
            wgu_war = [None, None]
            prev_gt = [None, None]
            prev_ft = [None, None]
            gt_war = [None, None]
            ft_war = [None, None]
            wd_war = [None, None]
            frd_war = [None, None]
            ost_war = [None, None]
            for j in range(NCH):
                yf_ev_box = []

                def em_yf(j=j):
                    def ld(e, j=j):
                        v = e.value_load(offs_s[0:1, j:j + 1])
                        src = ybag.ap()[j * D:(j + 1) * D, bass.ds(v, FT)] \
                            .rearrange("(hg p) t -> p hg t", p=128)
                        return e.dma_start(
                            out=a32[0:128, o_yf:o_yf + HG * FT].rearrange(
                                "p (hg t) -> p hg t", hg=HG), in_=src)
                    ev = P.add("sync", ld,
                               waits=[("Cc", j + 1)] + sl_yf_war[0], inc="Gf")
                    yf_ev_box.append(ev)
                yield (1.0, (lambda j=j: len(ag_evs) > j), em_yf)

                st_box = []

                def em_stats(q, src_v, pre, war23, box):
                    def em():
                        for hg in range(q * 4, q * 4 + 4):
                            evq = P.add("act", lambda e, hg=hg: e.activation(
                                sqv(hg % 2), src_v(hg), AF.Square),
                                waits=pre(hg) + [box[-1] if (hg >= 2 and box) else None],
                                inc="Ac")
                            P.add("pe", lambda e, hg=hg: e.matmul(
                                pb[2][0:1, 0:FT], one128_s[:], src_v(hg),
                                start=(hg == 0), stop=(hg == HG - 1)),
                                waits=pre(hg) + [ev_flush] + (war23 if hg == 0 else []))
                            ev2 = P.add("pe", lambda e, hg=hg: e.matmul(
                                pb[3][0:1, 0:FT], one128_s[:], sqv(hg % 2),
                                start=(hg == 0), stop=(hg == HG - 1)),
                                waits=[evq], inc="Pe")
                            box.append(ev2)
                    return em
                pre1 = lambda hg: [yf_ev_box[0]]
                for q in range(4):
                    yield (1.5, (lambda: True),
                           em_stats(q, yfv, pre1, sl_pb23_war[0], st_box))

                rows_box = []

                def em_rows(box, st_box, war67):
                    def em():
                        ev0 = P.add("dve", lambda e: e.tensor_scalar_mul(
                            row(0), pb[2][0:1, 0:FT], INV_D), waits=[st_box[-1]])
                        P.add("dve", lambda e: e.tensor_scalar_mul(row(3), pb[3][0:1, 0:FT], INV_D))
                        P.add("dve", lambda e: e.tensor_mul(row(2), row(0), row(0)))
                        P.add("dve", lambda e: e.tensor_sub(row(3), row(3), row(2)))
                        ev_var = P.add("dve", lambda e: e.tensor_scalar_add(
                            row(3), row(3), EPS_LN), inc="Dv")
                        ev_ln = P.add("act", lambda e: e.activation(row(2), row(3), AF.Ln),
                                      waits=[ev_var], inc="Ac")
                        ev_hf = P.add("dve", lambda e: e.tensor_scalar_mul(
                            row(2), row(2), -0.5), waits=[ev_ln], inc="Dv")
                        ev_rstd = P.add("act", lambda e: e.activation(row(1), row(2), AF.Exp),
                                        waits=[ev_hf], inc="Ac")
                        ev_b = P.add("dve", lambda e: e.scalar_tensor_tensor(
                            row(2), row(0), -1.0, row(1), ALU.mult, ALU.mult),
                            waits=[ev_rstd], inc="Dv")
                        eva = P.add("pe", lambda e: e.matmul(
                            pb[6][:, 0:FT], onec_s[:], row(1), start=True, stop=True),
                            waits=[ev_b, ev_flush] + war67, inc="Pe")
                        evb = P.add("pe", lambda e: e.matmul(
                            pb[7][:, 0:FT], onec_s[:], row(2), start=True, stop=True),
                            inc="Pe")
                        box.append(evb)
                    return em
                yield (1.0, (lambda: True), em_rows(rows_box, st_box, sl_pb67_war[0]))

                x1_box = []

                def em_x1(q, rows_box, x1b_war, box):
                    def em():
                        for hg in range(q * 4, q * 4 + 4):
                            P.add("dve", lambda e, hg=hg: e.tensor_mul(
                                yfv(hg), yfv(hg), pb[6][:, 0:FT]),
                                waits=[rows_box[0]])
                            P.add("dve", lambda e, hg=hg: e.tensor_add(
                                yfv(hg), yfv(hg), pb[7][:, 0:FT]))
                            ev = P.add("dve", lambda e, hg=hg: e.tensor_copy(
                                x1bv(hg), yfv(hg)),
                                waits=(x1b_war if hg == 0 else []), inc="Dv")
                            box.append(ev)
                    return em
                for q in range(4):
                    yield (4.0, (lambda: True), em_x1(q, rows_box, sl_x1b_war[0], x1_box))

                fdma_evs = [None] * NIT
                my_frd_prev = last_frd_ev[0]

                a67_box = []

                def em_A(it, x1_box, fdma_evs, my_frd_prev, a67_box=a67_box):
                    ii = it % 2
                    gb, ub = (2, 3) if ii == 0 else (6, 7)

                    def em():
                        evw = P.add("sync", lambda e, it=it, ii=ii: e.dma_start(
                            out=b16v(o_wgu[ii], 4096), in_=hgu_bf[it]),
                            waits=[("Gp", it + 1), wgu_war[ii]], inc=f"Gw{ii}")
                        evg = evu = None
                        for kg in range(DBA):
                            st, sp = kg == 0, kg == DBA - 1
                            evg = P.add("pe", lambda e, kg=kg, st=st, sp=sp, gb=gb, ii=ii: e.matmul(
                                pb[gb][:, 0:FT], wguv(ii, kg * 128, 128), x1bv(kg),
                                start=st, stop=sp),
                                waits=[evw, x1_box[-1]] + ([prev_gt[ii]] if st else []),
                                inc="Pe" if sp else None) or evg
                            evu = P.add("pe", lambda e, kg=kg, st=st, sp=sp, ub=ub, ii=ii: e.matmul(
                                pb[ub][:, 0:FT], wguv(ii, 2048 + kg * 128, 128), x1bv(kg),
                                start=st, stop=sp),
                                waits=([prev_ft[ii]] if st else []),
                                inc="Pe" if sp else None) or evu
                        wgu_war[ii] = evu
                        sl_x1b_war[0] = [evu]
                        evgt = P.add("act", lambda e, it=it, ii=ii, gb=gb: e.activation(
                            gtv(ii), pb[gb][:, 0:FT], AF.Silu, scale=sg_s[:, it:it + 1]),
                            waits=[evg, evu, gt_war[ii]], inc="Ac")
                        evft = P.add("dve", lambda e, it=it, ii=ii, ub=ub: e.scalar_tensor_tensor(
                            ftv(ii), pb[ub][:, 0:FT], su_s[:, it:it + 1], gtv(ii),
                            ALU.mult, ALU.mult),
                            waits=[evgt, ft_war[ii]], inc="Dv")
                        prev_gt[ii] = evgt
                        prev_ft[ii] = evft
                        gt_war[ii] = evft
                        evfd = P.add("sync", lambda e, it=it, ii=ii: e.dma_start(
                            out=fuse_d[it], in_=ftv(ii)),
                            waits=[evft, my_frd_prev[it]], inc=f"Lf{ii}")
                        fdma_evs[it] = evfd
                        ft_war[ii] = evfd
                        if it == NIT - 2:
                            sl_pb23_war[0] = [evgt, evft]
                        if it == NIT - 1:
                            a67_box.append(evgt)
                            a67_box.append(evft)
                    return em
                for it in range(NIT):
                    yield (4.0, (lambda: True), em_A(it, x1_box, fdma_evs, my_frd_prev))

                my_frd = [None] * NIT
                last_frd_ev[0] = my_frd
                evac_box = []

                def em_B(p, it0, fdma_evs, my_frd, evac_box):
                    def em():
                        for it in range(it0, it0 + 8):
                            ii = it % 2
                            evwd = P.add("sync", lambda e, it=it, ii=ii, p=p: e.dma_start(
                                out=b16v(o_wd[ii], 512),
                                in_=hd_bf[it, :, p * 512:(p + 1) * 512]),
                                waits=[("Gp", NIT + it + 1), wd_war[ii]], inc=f"Gv{ii}")
                            evfr = P.add("sync", lambda e, it=it, ii=ii: e.dma_start(
                                out=frdv(ii), in_=fuse_d[it]),
                                waits=[fdma_evs[it], frd_war[ii]], inc=f"Lr{ii}")
                            evd = None
                            for g in range(4):
                                st, sp = it == 0, it == NIT - 1
                                w0 = [evwd, evfr]
                                if st:
                                    w0 = w0 + [pbB_war[g]]
                                    if g >= 2:
                                        w0 = w0 + a67_box
                                evd = P.add("pe", lambda e, it=it, ii=ii, g=g, st=st, sp=sp: e.matmul(
                                    pb[4 + g][:, 0:FT], wdv(ii, g * 128, 128), frdv(ii),
                                    start=st, stop=sp), waits=w0,
                                    inc="Pe" if (g == 3 or sp) else None) or evd
                            wd_war[ii] = evd
                            frd_war[ii] = evd
                            my_frd[it] = evd
                            if it == NIT - 1:
                                for g in range(4):
                                    hg = p * 4 + g
                                    eve = P.add("dve", lambda e, g=g, hg=hg: e.scalar_tensor_tensor(
                                        yfv(hg), pb[4 + g][:, 0:FT], sd_s[:, hg:hg + 1],
                                        yfv(hg), ALU.mult, ALU.add),
                                        waits=[evd], inc="Dv")
                                    pbB_war[g] = eve
                                    evac_box.append(eve)
                    return em
                for p in range(4):
                    for it0 in range(0, NIT, 8):
                        yield (4.5, (lambda: True), em_B(p, it0, fdma_evs, my_frd, evac_box))

                st2_box = []
                pre2 = lambda hg: [evac_box[hg // 4 * 4 + hg % 4]]
                for q in range(4):
                    yield (1.5, (lambda: True),
                           em_stats(q, yfv, pre2, sl_pb23_war[0], st2_box))
                rows2_box = []
                yield (1.0, (lambda: True), em_rows(rows2_box, st2_box, [evac_box[-1]]))

                out_box = []

                def em_out(q, rows2_box, out_box, j=j):
                    def em():
                        for hg in range(q * 4, q * 4 + 4):
                            P.add("dve", lambda e, hg=hg: e.tensor_mul(
                                ostv(hg % 2), yfv(hg), pb[6][:, 0:FT]),
                                waits=[rows2_box[0], ost_war[hg % 2]])
                            evo = P.add("dve", lambda e, hg=hg: e.tensor_add(
                                ostv(hg % 2), ostv(hg % 2), pb[7][:, 0:FT]), inc="Dv")
                            evod = P.add("sync", lambda e, hg=hg, j=j: e.dma_start(
                                out=out[hg * 128:(hg + 1) * 128, j * FT:(j + 1) * FT],
                                in_=ostv(hg % 2)),
                                waits=[evo], inc=f"Lo{hg % 2}")
                            ost_war[hg % 2] = evod
                            out_box.append(evo)
                    return em
                for q in range(4):
                    yield (2.5, (lambda: True), em_out(q, rows2_box, out_box))

                def em_seal(out_box, st2_box):
                    def em():
                        sl_yf_war[0] = out_box[-4:]
                        sl_pb67_war[0] = out_box[-4:]
                        sl_pb23_war[0] = [st2_box[-1]]
                    return em
                yield (0.0, (lambda: True), em_seal(out_box, st2_box))

        class Puller:
            def __init__(self, gen):
                self.gen = gen
                self.pending = None
                self.done = False

            def pull(self, budget):
                while not self.done:
                    if self.pending is None:
                        try:
                            self.pending = next(self.gen)
                        except StopIteration:
                            self.done = True
                            break
                    cost, ready, emit = self.pending
                    if not ready():
                        break
                    if cost > budget:
                        break
                    emit()
                    budget -= cost
                    self.pending = None

            def drain(self):
                while not self.done:
                    self.pull(1e9)
                    if not self.done:
                        break

        puller = Puller(ffn_stream() if PART == "all" else iter(()))

        xt_war = [None] * 3
        acc_war = [None] * 2
        dA_war = [None] * 2
        u_war = [None] * 2
        dbx_war = [None] * 2
        h_war = [[], []]
        bsb_war = [None] * 2
        csb_war = [None] * 2
        psdt_war = [None, None]
        dt_ready = [None] * DBM
        y_init_ev = [None] * DBM
        chunk_bar = []
        prev_dtmm_last = None
        prev_bcd_last = None
        prev_abs_ev = None
        bcast_last = []
        ev_bcB = [None] * S
        ev_bcC = [None] * S

        for tau in range(NCH):
            t0 = tau * TC
            xc_ready = [None] * DBA
            for db in range(DBA):
                it4 = db % 3
                ld_ev = P.add("sync", lambda e, i=it4, r0=db * 128, c0=t0: e.dma_start(
                    out=xtv(i), in_=xT[r0:r0 + 128, c0:c0 + TC + 3]),
                    waits=[xt_war[it4]] + (chunk_bar if db < 4 else []), inc=f"Lx{it4}")
                i2 = db % 2
                P.add("dve", lambda e, i2=i2, i=it4, db=db: e.tensor_scalar(
                    accv(i2), xts(i, 0, TC), cw_s[:, db * 4:db * 4 + 1], None, ALU.mult),
                    waits=[ld_ev, acc_war[i2], evA] + (chunk_bar if db < 2 else []))
                P.add("dve", lambda e, i2=i2, i=it4, db=db: e.scalar_tensor_tensor(
                    acc2v(i2), xts(i, 1, TC), cw_s[:, db * 4 + 1:db * 4 + 2],
                    accv(i2), ALU.mult, ALU.add))
                P.add("dve", lambda e, i2=i2, i=it4, db=db: e.scalar_tensor_tensor(
                    accv(i2), xts(i, 2, TC), cw_s[:, db * 4 + 2:db * 4 + 3],
                    acc2v(i2), ALU.mult, ALU.add))
                tap_ev = P.add("dve", lambda e, i2=i2, i=it4, db=db: e.scalar_tensor_tensor(
                    acc2v(i2), xts(i, 3, TC), cw_s[:, db * 4 + 3:db * 4 + 4],
                    accv(i2), ALU.mult, ALU.add), inc="Dv")
                if db < DBM:
                    sl_ev = P.add("act", lambda e, i2=i2, db=db: e.activation(
                        xcm(db), acc2v(i2), AF.Silu, bias=cb_s[:, db:db + 1]),
                        waits=[tap_ev] + (chunk_bar if db < 2 else []), inc="Ac")
                    cp_ev = P.add("dve", lambda e, db=db: e.tensor_copy(xcb(db), xcm(db)),
                                  waits=[sl_ev], inc="Dv")
                    yi_ev = P.add("dve", lambda e, db=db, i=it4: e.scalar_tensor_tensor(
                        yv(db), xcm(db), dp_s[:, db:db + 1], xts(i, 3, TC),
                        ALU.mult, ALU.add), inc="Dv")
                    y_init_ev[db] = yi_ev
                    xc_ready[db] = cp_ev
                    xt_war[it4] = yi_ev
                    acc_war[i2] = cp_ev
                else:
                    sl_ev = P.add("act", lambda e, i2=i2, db=db: e.activation(
                        xcb(db), acc2v(i2), AF.Silu, bias=cb_s[:, db:db + 1]),
                        waits=[tap_ev], inc="Ac")
                    xc_ready[db] = sl_ev
                    xt_war[it4] = tap_ev
                    acc_war[i2] = sl_ev
                if db == 7:
                    pump_casts()
                    puller.pull(PACE)
            proj_ev = None
            for kg in range(DBA):
                st, sp = kg == 0, kg == DBA - 1
                P.add("pe", lambda e, kg=kg, st=st, sp=sp: e.matmul(
                    pb[0][:, :], wxp_s[:, kg * 160:kg * 160 + 128], xcb(kg), start=st, stop=sp),
                    waits=[xc_ready[kg], ev_wdt, ev_wxp]
                    + ([prev_dtmm_last] + chunk_bar if st else []),
                    inc="Pe" if sp else None)
                proj_ev = P.add("pe", lambda e, kg=kg, st=st, sp=sp: e.matmul(
                    pb[1][0:32, :], wxp_s[:, kg * 160 + 128:kg * 160 + 160], xcb(kg),
                    start=st, stop=sp), inc="Pe" if sp else None) or proj_ev
            ev_dtr = P.add("act", lambda e: e.copy(dtr_s[:], pb[0][:, :]),
                           waits=[proj_ev, prev_dtmm_last], inc="Ac")
            ev_bc = P.add("act", lambda e: e.copy(bc_s[:], pb[1][0:32, :]),
                          waits=[proj_ev, prev_bcd_last], inc="Ac")
            ev_bcd = P.add("sync", lambda e, tau=tau: e.dma_start(
                out=bcd[tau % 2].ap(), in_=bc_s[:]),
                waits=[ev_bc] + (bcast_last[tau - 2] if tau >= 2 else []), inc="Lbc")
            prev_bcd_last = ev_bcd
            for db in range(DBM):
                i2 = db % 2
                mm_ev = P.add("pe", lambda e, db=db, i2=i2: e.matmul(
                    pb[i2][:, :], wdt_s[:, db * 128:(db + 1) * 128], dtr_s[:],
                    start=True, stop=True),
                    waits=[ev_dtr, ev_bcd, psdt_war[i2]], inc="Pe")
                ev_usb = P.add("dve", lambda e, db=db, i2=i2: e.tensor_scalar_add(
                    usbv, pb[i2][:, :], dtb_s[:, db:db + 1]),
                    waits=[mm_ev, prev_abs_ev], inc="Dv")
                P.add("dve", lambda e: e.tensor_scalar_max(rlv, usbv, 0.0))
                ev_abs = P.add("act", lambda e: e.activation(sp0v, usbv, AF.Abs),
                               waits=[ev_usb], inc="Ac")
                P.add("act", lambda e: e.activation(sp1v, sp0v, AF.Exp, scale=-1.0))
                ab_ev = P.add("act", lambda e: e.activation(
                    sp0v, sp1v, AF.Ln, bias=1.0), inc="Ac")
                dt_ev = P.add("dve", lambda e, db=db: e.scalar_tensor_tensor(
                    dtv(db), rlv, 1.0, sp0v, ALU.mult, ALU.add),
                    waits=[ab_ev], inc="Dv")
                P.add("dve", lambda e, db=db: e.scalar_tensor_tensor(
                    wv_(db), dtv(db), 1.0, xcm(db), ALU.mult, ALU.mult), inc="Dv")
                dt_ready[db] = dt_ev
                prev_abs_ev = dt_ev
                psdt_war[i2] = ev_usb
            pump_casts()
            puller.pull(PACE)
            for s in range(2):
                ev_bcB[s] = P.add("act", lambda e, s=s, tau=tau: e.dma_start(
                    out=bsbv(s % 2),
                    in_=bcd[tau % 2].ap()[s:s + 1, :].broadcast_to([128, TC])),
                    waits=[ev_bcd, bsb_war[s % 2]], inc=f"Lb{s % 2}")
                ev_bcC[s] = P.add("act", lambda e, s=s, tau=tau: e.dma_start(
                    out=csbv(s % 2),
                    in_=bcd[tau % 2].ap()[16 + s:17 + s, :].broadcast_to([128, TC])),
                    waits=[ev_bcd, csb_war[s % 2]], inc=f"Lb{2 + s % 2}")
            scan_ev = None
            last_evx = None
            yadd_last = None
            NI = S * DBM
            dA_evs = [None] * NI
            u_evs = [None] * NI

            def emit_front(j, tau=tau):
                s, db = j // DBM, j % DBM
                jj = j % 2
                dA_evs[j] = P.add("act", lambda e, db=db, jj=jj, col=db * 16 + s: e.activation(
                    dAx(jj), dtv(db), AF.Exp, scale=A_s[:, col:col + 1]),
                    waits=[dt_ready[db], dA_war[jj], ev_ms[jj], ev_st0], inc="Ac")
                ev_u = P.add("dve", lambda e, db=db, jj=jj, s2=s % 2: e.scalar_tensor_tensor(
                    uv(jj), wv_(db), 1.0, bsbv(s2), ALU.mult, ALU.mult),
                    waits=[ev_bcB[s], u_war[jj]], inc="Dv")
                P.add("act", lambda e, jj=jj: e.activation(
                    dbxx(jj), uv(jj), AF.Relu), waits=[ev_u, dbx_war[jj]])
                u_evs[j] = P.add("act", lambda e, jj=jj, i=j: e.copy(
                    dbx0(jj), state[:, i:i + 1]), inc="Ac")

            emit_front(0)
            for i in range(NI):
                s, db = i // DBM, i % DBM
                ii = i % 2
                if i + 1 < NI:
                    emit_front(i + 1)
                evh = P.add("dve", lambda e, ii=ii: e.tensor_tensor_scan(
                    hv(ii), dAv(ii), dbxv(ii), 0.0, ALU.mult, ALU.add),
                    waits=[dA_evs[i], u_evs[i]] + h_war[ii], inc="Dv")
                scan_ev = evh
                evx = P.add("act", lambda e, ii=ii, i=i: e.copy(
                    state[:, i:i + 1], hv(ii)[:, TC:TC + 1]), waits=[evh], inc="Ac")
                last_evx = evx
                if i % 2 == 0:
                    evv = P.add("dve", lambda e, ii=ii, s2=s % 2: e.scalar_tensor_tensor(
                        vv(ii), hx(ii), 1.0, csbv(s2), ALU.mult, ALU.mult),
                        waits=[ev_bcC[s]], inc="Dv")
                else:
                    evv = P.add("pool", lambda e, ii=ii, s2=s % 2: e.tensor_mul(
                        vv(ii), hx(ii), csbv(s2)),
                        waits=[evh, ev_bcC[s]], inc="Po")
                evy = P.add("pool", lambda e, ii=ii, db=db: e.tensor_add(
                    yv(db), yv(db), vv(ii)),
                    waits=[evv] + ([y_init_ev[db]] if s == 0 else []), inc="Po")
                yadd_last = evy
                dA_war[ii] = evy
                u_war[ii] = evh
                dbx_war[ii] = evh
                h_war[ii] = [evy, evx]
                if db == DBM - 1:
                    bsb_war[s % 2] = u_evs[i]
                    csb_war[s % 2] = evv
                    if s + 2 < S:
                        sn = s + 2
                        ev_bcB[sn] = P.add("act", lambda e, sn=sn, tau=tau: e.dma_start(
                            out=bsbv(sn % 2),
                            in_=bcd[tau % 2].ap()[sn:sn + 1, :].broadcast_to([128, TC])),
                            waits=[ev_bcd, bsb_war[sn % 2]], inc=f"Lb{sn % 2}")
                        ev_bcC[sn] = P.add("act", lambda e, sn=sn, tau=tau: e.dma_start(
                            out=csbv(sn % 2),
                            in_=bcd[tau % 2].ap()[16 + sn:17 + sn, :].broadcast_to([128, TC])),
                            waits=[ev_bcd, csb_war[sn % 2]], inc=f"Lb{2 + sn % 2}")
                    puller.pull(PACE)
            prev_dtmm_last = psdt_war[1]
            bcast_last.append([ev_bcB[14], ev_bcB[15], ev_bcC[14], ev_bcC[15]])
            yb_ev = P.add("sync", lambda e, tau=tau: e.dma_start(
                out=ybin[tau].ap().rearrange("(db p) t -> p db t", p=128),
                in_=a32[:, o_y:o_y + DBM * TC].rearrange("p (db t) -> p db t", db=DBM)),
                waits=[yadd_last], inc=f"Ly{tau}")
            ag_ev = P.add("pool", lambda e, tau=tau: e.collective_compute(
                "AllGather", ALU.bypass, replica_groups=PAIRS,
                ins=[ybin[tau].ap().opt()],
                outs=[ybag.ap()[tau * D:(tau + 1) * D, :].opt()]),
                waits=[yb_ev], inc="Cc")
            ag_evs.append(ag_ev)
            chunk_bar = [scan_ev, yadd_last, yb_ev, proj_ev, ev_dtr, ev_bc, last_evx]

        while not puller.done:
            puller.pull(1e9)

        if PART == "mixer":
            prev = ag_evs[-1]
            for q in range(2 * HG):
                eq = P.add("pool", lambda e, q=q: e.dma_start(
                    out=f32v(o_yf + (q % HG) * FT, FT),
                    in_=ybag.ap()[q * 128:(q + 1) * 128, 0:FT]),
                    waits=[prev], inc="Gf")
                prev = P.add("sync", lambda e, q=q: e.dma_start(
                    out=out[(q % HG) * 128:((q % HG) + 1) * 128,
                            (q // HG) * FT:(q // HG + 1) * FT],
                    in_=f32v(o_yf + (q % HG) * FT, FT)), waits=[eq], inc="Lo0")

        P.add("sync", lambda e: e.nop(),
              waits=[(s, P.cnt[s]) for s in list(P.cnt) if is_dma_sem(s)])

        with nc.Block() as block:
            @block.sync
            def _(eng):
                P.emit_engine("sync", eng, sems)

            @block.tensor
            def _(eng):
                P.emit_engine("pe", eng, sems)

            @block.scalar
            def _(eng):
                P.emit_engine("act", eng, sems)

            @block.vector
            def _(eng):
                P.emit_engine("dve", eng, sems)

            @block.gpsimd
            def _(eng):
                P.emit_engine("pool", eng, sems)

    return nc


_NC_CACHE = {}


def _get_nc():
    if "nc" not in _NC_CACHE:
        _NC_CACHE["nc"] = build_nc()
    return _NC_CACHE["nc"]


def kernel(L0_attn_x, L0_attn_conv_w, L0_attn_conv_b, L0_attn_x_proj_w,
           L0_attn_dt_proj_w, L0_attn_dt_proj_b, L0_attn_A_log, L0_attn_D_param,
           L0_ffn_w_gate, L0_ffn_w_up, L0_ffn_w_down, L0_ffn_s_gate,
           L0_ffn_s_up, L0_ffn_s_down):
    x = np.asarray(L0_attn_x, np.float32)
    conv_w = np.asarray(L0_attn_conv_w, np.float32)
    conv_b = np.asarray(L0_attn_conv_b, np.float32)
    x_proj_w = np.asarray(L0_attn_x_proj_w, np.float32)
    dt_proj_w = np.asarray(L0_attn_dt_proj_w, np.float32)
    dt_proj_b = np.asarray(L0_attn_dt_proj_b, np.float32)
    A_log = np.asarray(L0_attn_A_log, np.float32)
    D_param = np.asarray(L0_attn_D_param, np.float32)
    wg8 = np.asarray(L0_ffn_w_gate)
    wu8 = np.asarray(L0_ffn_w_up)
    wd8 = np.asarray(L0_ffn_w_down)
    s_gate = np.asarray(L0_ffn_s_gate, np.float32)
    s_up = np.asarray(L0_ffn_s_up, np.float32)
    s_down = np.asarray(L0_ffn_s_down, np.float32)

    def half_prep(role):
        lo = role * 1024
        perm = np.concatenate([np.arange(lo, lo + 1024),
                               np.arange((1 - role) * 1024, (1 - role) * 1024 + 1024)])
        d = {}
        d["cw"] = np.ascontiguousarray(
            conv_w[perm, 0, :].reshape(16, 128, 4).transpose(1, 0, 2).reshape(128, 64))
        d["cbi"] = np.ascontiguousarray(conv_b[perm].reshape(16, 128).T)
        d["dpi"] = np.ascontiguousarray(D_param[lo:lo + 1024].reshape(8, 128).T)
        d["dtbi"] = np.ascontiguousarray(dt_proj_b[lo:lo + 1024].reshape(8, 128).T)
        d["alog"] = np.ascontiguousarray(
            A_log[lo:lo + 1024].reshape(8, 128, 16).transpose(1, 0, 2).reshape(128, 128))
        d["wxp"] = np.ascontiguousarray(x_proj_w.T[perm])
        d["wdt"] = np.ascontiguousarray(dt_proj_w[lo:lo + 1024].T)
        d["offs"] = np.full((1, 4), role * FT, np.uint32)
        return d, perm

    halves = [half_prep(0), half_prep(1)]

    onescol = np.ones((1, 128), np.float32)
    ones128 = np.ones((128, 1), np.float32)
    g4 = wg8.reshape(16, 128, 64, 128).transpose(2, 1, 0, 3)
    u4 = wu8.reshape(16, 128, 64, 128).transpose(2, 1, 0, 3)
    hgu = np.ascontiguousarray(np.stack([g4, u4], axis=2).reshape(NIT, 128, 4096))
    hd = np.ascontiguousarray(wd8.reshape(NIT, 128, 2048))
    sgl = np.ascontiguousarray(s_gate.reshape(64, 128).T)
    sul = np.ascontiguousarray(s_up.reshape(64, 128).T)
    sdl = np.ascontiguousarray(s_down.reshape(16, 128).T)

    in_maps = []
    for c in range(8):
        b, role = c // 2, c % 2
        d, perm = halves[role]
        xTh = np.zeros((D, T + 3), np.float32)
        xTh[:, 3:] = x[b].T[perm]
        m = {"xT": xTh, "onescol": onescol, "ones128": ones128,
             "hgu": hgu, "hd": hd, "sgd": sgl, "sud": sul, "sdd": sdl}
        m.update(d)
        in_maps.append(m)

    nc = _get_nc()
    res = run_bass_kernel_spmd(nc, in_maps, core_ids=list(range(8)))

    result = np.empty((BATCH, T, D), np.float32)
    for c in range(8):
        b, role = c // 2, c % 2
        o = np.asarray(res.results[c]["out"])
        for j in range(NCH):
            result[b, j * TC + role * FT: j * TC + (role + 1) * FT, :] = \
                o[:, j * FT:(j + 1) * FT].T
    return result


# version 15
# speedup vs baseline: 1.0037x; 1.0037x over previous
import contextlib
import numpy as np
import ml_dtypes
import concourse.bass as bass
import concourse.mybir as mybir
from concourse.bass_utils import run_bass_kernel_spmd

F32 = mybir.dt.float32
BF16 = mybir.dt.bfloat16
I8 = mybir.dt.int8
U32 = mybir.dt.uint32
AF = mybir.ActivationFunctionType
ALU = mybir.AluOpType

D = 2048
T = 2048
BATCH = 4
TC = 512
NCH = T // TC
FT = 256
DBA = 16
DBM = 8
S = 16
RK = 128
NIT = 64
HG = 16
INV_D = 1.0 / D
EPS_LN = 1e-5
PAIRS = [[0, 1], [2, 3], [4, 5], [6, 7]]
PACE = 26.0

ENGINES = ("sync", "pe", "act", "dve", "pool")
OWN_SEM = {"pe": "Pe", "act": "Ac", "dve": "Dv", "pool": "Po"}
def is_dma_sem(s):
    return s[0] in ("L", "G")


class Prog:

    def __init__(self):
        self.ops = []
        self.cnt = {}

    def ev_last(self, sem):
        return (sem, self.cnt.get(sem, 0))

    def add(self, eng, fn, waits=(), inc=None):
        ev = None
        if inc is not None:
            self.cnt[inc] = self.cnt.get(inc, 0) + 1
            ev = (inc, self.cnt[inc])
        self.ops.append((eng, fn, tuple(w for w in waits if w is not None), inc))
        return ev

    def emit_engine(self, name, eng, sems):
        known = {}
        own = OWN_SEM.get(name)
        for oeng, fn, waits, inc in self.ops:
            if oeng != name:
                continue
            for sem, val in waits:
                if sem.endswith("!"):
                    sem = sem[:-1]
                elif sem == own:
                    continue
                tgt = val * 16 if is_dma_sem(sem) else val
                if known.get(sem, 0) < tgt:
                    eng.wait_ge(sems[sem], tgt)
                    known[sem] = tgt
            ins = fn(eng)
            if inc is not None:
                ins.then_inc(sems[inc], 16 if is_dma_sem(inc) else 1)


PART = "all"


def build_nc():
    nc = bass.Bass("TRN2", target_bir_lowering=False, debug=False)

    xT = nc.dram_tensor("xT", [D, T + 3], F32, kind="ExternalInput").ap()
    cw = nc.dram_tensor("cw", [128, 64], F32, kind="ExternalInput").ap()
    cbi = nc.dram_tensor("cbi", [128, 16], F32, kind="ExternalInput").ap()
    dpi = nc.dram_tensor("dpi", [128, 8], F32, kind="ExternalInput").ap()
    dtbi = nc.dram_tensor("dtbi", [128, 8], F32, kind="ExternalInput").ap()
    alog = nc.dram_tensor("alog", [128, 128], F32, kind="ExternalInput").ap()
    wxp = nc.dram_tensor("wxp", [D, 160], F32, kind="ExternalInput").ap()
    wdt = nc.dram_tensor("wdt", [RK, 1024], F32, kind="ExternalInput").ap()
    onescol = nc.dram_tensor("onescol", [1, 128], F32, kind="ExternalInput").ap()
    ones128 = nc.dram_tensor("ones128", [128, 1], F32, kind="ExternalInput").ap()
    hgu = nc.dram_tensor("hgu", [NIT, 128, 4096], I8, kind="ExternalInput").ap()
    hd = nc.dram_tensor("hd", [NIT, 128, 2048], I8, kind="ExternalInput").ap()
    sgd = nc.dram_tensor("sgd", [128, 64], F32, kind="ExternalInput").ap()
    sud = nc.dram_tensor("sud", [128, 64], F32, kind="ExternalInput").ap()
    sdd = nc.dram_tensor("sdd", [128, 16], F32, kind="ExternalInput").ap()
    offs = nc.dram_tensor("offs", [1, 4], U32, kind="ExternalInput").ap()

    out = nc.dram_tensor("out", [D, NCH * FT], F32, kind="ExternalOutput").ap()

    ybin = [nc.dram_tensor(f"ybin{t}", [1024, TC], F32) for t in range(NCH)]
    ybag = nc.dram_tensor("ybag", [NCH * D, TC], F32)
    bcd = [nc.dram_tensor(f"bcd{i}", [32, TC], F32) for i in range(2)]
    fuse_d = nc.dram_tensor("fuse_d", [NIT, 128, FT], BF16)
    hgu_bf = nc.dram_tensor("hgu_bf", [NIT, 128, 4096], BF16)
    hd_bf = nc.dram_tensor("hd_bf", [NIT, 128, 2048], BF16)

    ctx = contextlib.ExitStack()
    sb = lambda name, shape, dt: ctx.enter_context(nc.sbuf_tensor(name, shape, dt))
    psb = lambda name, shape: ctx.enter_context(nc.psum_tensor(name, shape, F32))

    with ctx:
        cw_s = sb("cw_s", [128, 64], F32)
        cb_s = sb("cb_s", [128, 16], F32)
        dp_s = sb("dp_s", [128, 8], F32)
        dtb_s = sb("dtb_s", [128, 8], F32)
        A_s = sb("A_s", [128, 128], F32)
        Atmp_s = sb("Atmp_s", [128, 128], F32)
        wxp_s = sb("wxp_s", [128, 16 * 160], BF16)
        wdt_s = sb("wdt_s", [128, 1024], BF16)
        onec_s = sb("onec_s", [1, 128], F32)
        one128_s = sb("one128_s", [128, 1], F32)
        sg_s = sb("sg_s", [128, 64], F32)
        su_s = sb("su_s", [128, 64], F32)
        sd_s = sb("sd_s", [128, 16], F32)
        offs_s = sb("offs_s", [1, 4], U32)
        state = sb("state", [128, 128], F32)
        dtr_s = sb("dtr_s", [128, TC], BF16)
        bc_s = sb("bc_s", [32, TC], F32)

        AB = 185 * 1024
        arena = sb("arena", [128, AB], I8)
        a32 = arena.bitcast(F32)
        a16 = arena.bitcast(BF16)

        class Alloc:
            def __init__(self):
                self.cur = 0

            def f32(self, nelem):
                off = (self.cur + 63) // 64 * 64
                self.cur = off + nelem * 4
                assert self.cur <= AB, f"arena overflow {self.cur}"
                return off // 4

            def bf16(self, nelem):
                off = (self.cur + 63) // 64 * 64
                self.cur = off + nelem * 2
                assert self.cur <= AB, f"arena overflow {self.cur}"
                return off // 2

        M = Alloc()
        o_xt = [M.f32(TC + 3) for _ in range(3)]
        o_acc = [M.f32(TC) for _ in range(2)]
        o_acc2 = [M.f32(TC) for _ in range(2)]
        o_xcm = M.f32(DBM * TC)
        o_xcb = M.bf16(DBA * TC)
        o_dt = M.f32(DBM * TC)
        o_w = M.f32(DBM * TC)
        o_y = M.f32(DBM * TC)
        o_dA = [M.f32(TC + 16) for _ in range(2)]
        o_u = [M.f32(TC) for _ in range(2)]
        o_dbx = [M.f32(TC + 16) for _ in range(2)]
        o_h = [M.f32(TC + 16) for _ in range(2)]
        o_v = [M.f32(TC) for _ in range(2)]
        o_bsb = [M.f32(TC) for _ in range(2)]
        o_csb = [M.f32(TC) for _ in range(2)]
        o_usb = M.f32(TC)
        o_rl = M.f32(TC)
        o_sp0 = M.f32(TC)
        o_sp1 = M.f32(TC)
        o_yf = M.f32(HG * FT)
        o_x1b = M.bf16(HG * FT)
        o_wgu = [M.bf16(4096) for _ in range(2)]
        o_wd = [M.bf16(512) for _ in range(2)]
        o_frd = [M.bf16(FT) for _ in range(2)]
        o_gt = [M.f32(FT) for _ in range(2)]
        o_ft = [M.bf16(FT) for _ in range(2)]
        o_sq = [M.f32(FT) for _ in range(2)]
        o_rows = M.f32(4 * FT)
        o_ost = [M.f32(FT) for _ in range(2)]

        f32v = lambda off, n: a32[0:128, off:off + n]
        b16v = lambda off, n: a16[0:128, off:off + n]

        xtv = lambda i: f32v(o_xt[i], TC + 3)
        xts = lambda i, a, n: f32v(o_xt[i] + a, n)
        accv = lambda i: f32v(o_acc[i], TC)
        acc2v = lambda i: f32v(o_acc2[i], TC)
        xcm = lambda db: f32v(o_xcm + db * TC, TC)
        xcb = lambda db: b16v(o_xcb + db * TC, TC)
        dtv = lambda db: f32v(o_dt + db * TC, TC)
        wv_ = lambda db: f32v(o_w + db * TC, TC)
        yv = lambda db: f32v(o_y + db * TC, TC)
        dAv = lambda i: f32v(o_dA[i], TC + 1)
        dAx = lambda i: f32v(o_dA[i] + 1, TC)
        uv = lambda i: f32v(o_u[i], TC)
        dbxv = lambda i: f32v(o_dbx[i], TC + 1)
        dbxx = lambda i: f32v(o_dbx[i] + 1, TC)
        dbx0 = lambda i: f32v(o_dbx[i], 1)
        hv = lambda i: f32v(o_h[i], TC + 1)
        hx = lambda i: f32v(o_h[i] + 1, TC)
        vv = lambda i: f32v(o_v[i], TC)
        bsbv = lambda i: f32v(o_bsb[i], TC)
        csbv = lambda i: f32v(o_csb[i], TC)
        usbv = f32v(o_usb, TC)
        rlv = f32v(o_rl, TC)
        sp0v = f32v(o_sp0, TC)
        sp1v = f32v(o_sp1, TC)

        yfv = lambda hg: f32v(o_yf + hg * FT, FT)
        x1bv = lambda hg: b16v(o_x1b + hg * FT, FT)
        wguv = lambda i, a, n: b16v(o_wgu[i] + a, n)
        wdv = lambda i, a, n: b16v(o_wd[i] + a, n)
        frdv = lambda i: b16v(o_frd[i], FT)
        gtv = lambda i: f32v(o_gt[i], FT)
        ftv = lambda i: b16v(o_ft[i], FT)
        sqv = lambda i: f32v(o_sq[i], FT)
        row = lambda i: a32[0:1, o_rows + i * FT: o_rows + (i + 1) * FT]
        ostv = lambda i: f32v(o_ost[i], FT)

        pb = [psb(f"pb{i}", [128, 512]) for i in range(8)]

        sem_names = (["Lc", "Gc", "Lbc", "Cc", "Gf", "Gp", "Pe", "Ac", "Dv", "Po"]
                     + [f"Lb{i}" for i in range(4)]
                     + [f"Lx{i}" for i in range(3)] + [f"Ly{i}" for i in range(NCH)]
                     + [f"Gw{i}" for i in range(2)] + [f"Gv{i}" for i in range(2)]
                     + [f"Lr{i}" for i in range(2)] + [f"Lf{i}" for i in range(2)]
                     + [f"Lo{i}" for i in range(2)])
        sems = {n: ctx.enter_context(nc.semaphore(n)) for n in sem_names}

        P = Prog()

        const_loads = [("cw", cw_s, cw), ("cb", cb_s, cbi), ("dp", dp_s, dpi),
                       ("dtb", dtb_s, dtbi), ("alog", Atmp_s, alog),
                       ("onec", onec_s, onescol),
                       ("one128", one128_s, ones128), ("sg", sg_s, sgd),
                       ("su", su_s, sud), ("sd", sd_s, sdd), ("offs", offs_s, offs)]
        const_evs = {}
        for name, dst, src in const_loads:
            const_evs[name] = P.add("sync", lambda e, d=dst, s=src: e.dma_start(out=d[:], in_=s), inc="Lc")
        ev_wxp = P.add("pool", lambda e: e.dma_start(
            out=wxp_s.rearrange("p (kg f) -> p kg f", kg=16),
            in_=wxp.rearrange("(kg p) f -> p kg f", p=128)), inc="Gc")
        ev_wdt = P.add("pool", lambda e: e.dma_start(out=wdt_s[:], in_=wdt), inc="Gc")
        ev_flush = P.add("sync", lambda e: e.dma_start(out=Atmp_s[:], in_=alog), inc="Lc")
        evA1 = P.add("act", lambda e: e.activation(A_s[:], Atmp_s[:], AF.Exp),
                     waits=[ev_flush, const_evs["alog"]], inc="Ac")
        evA = P.add("dve", lambda e: e.tensor_scalar_mul(A_s[:], A_s[:], -1.0),
                    waits=[evA1], inc="Dv")
        ev_st0 = P.add("dve", lambda e: e.memset(state[:], 0.0), inc="Dv")
        ev_ms = []
        for i in range(2):
            ev_ms.append(P.add("dve", lambda e, i=i: e.memset(
                f32v(o_dA[i], 1), 0.0), inc="Dv"))
        for it in range(NIT):
            P.add("pool", lambda e, it=it: e.dma_start(
                out=hgu_bf[it], in_=hgu[it]), inc="Gp")
        for it in range(NIT):
            P.add("pool", lambda e, it=it: e.dma_start(
                out=hd_bf[it], in_=hd[it]), inc="Gp")

        def pump_casts(n=6):
            pass

        ag_evs = []
        sl_yf_war = [[]]
        sl_pb23_war = [[]]
        sl_pb67_war = [[]]
        sl_x1b_war = [[]]
        pbB_war = [None] * 4
        last_frd_ev = [[None] * NIT]

        def ffn_stream():
            wgu_war = [None, None]
            prev_gt = [None, None]
            prev_ft = [None, None]
            gt_war = [None, None]
            ft_war = [None, None]
            wd_war = [None, None]
            frd_war = [None, None]
            ost_war = [None, None]
            for j in range(NCH):
                yf_ev_box = []

                def em_yf(j=j):
                    def ld(e, j=j):
                        v = e.value_load(offs_s[0:1, j:j + 1])
                        src = ybag.ap()[j * D:(j + 1) * D, bass.ds(v, FT)] \
                            .rearrange("(hg p) t -> p hg t", p=128)
                        return e.dma_start(
                            out=a32[0:128, o_yf:o_yf + HG * FT].rearrange(
                                "p (hg t) -> p hg t", hg=HG), in_=src)
                    ev = P.add("sync", ld,
                               waits=[("Cc", j + 1)] + sl_yf_war[0], inc="Gf")
                    yf_ev_box.append(ev)
                yield (1.0, (lambda j=j: len(ag_evs) > j), em_yf)

                st_box = []

                def em_stats(q, src_v, pre, war23, box):
                    def em():
                        for hg in range(q * 4, q * 4 + 4):
                            evq = P.add("act", lambda e, hg=hg: e.activation(
                                sqv(hg % 2), src_v(hg), AF.Square),
                                waits=pre(hg) + [box[-1] if (hg >= 2 and box) else None],
                                inc="Ac")
                            P.add("pe", lambda e, hg=hg: e.matmul(
                                pb[2][0:1, 0:FT], one128_s[:], src_v(hg),
                                start=(hg == 0), stop=(hg == HG - 1)),
                                waits=pre(hg) + [ev_flush] + (war23 if hg == 0 else []))
                            ev2 = P.add("pe", lambda e, hg=hg: e.matmul(
                                pb[3][0:1, 0:FT], one128_s[:], sqv(hg % 2),
                                start=(hg == 0), stop=(hg == HG - 1)),
                                waits=[evq], inc="Pe")
                            box.append(ev2)
                    return em
                pre1 = lambda hg: [yf_ev_box[0]]
                for q in range(4):
                    yield (1.5, (lambda: True),
                           em_stats(q, yfv, pre1, sl_pb23_war[0], st_box))

                rows_box = []

                def em_rows(box, st_box, war67):
                    def em():
                        ev0 = P.add("dve", lambda e: e.tensor_scalar_mul(
                            row(0), pb[2][0:1, 0:FT], INV_D), waits=[st_box[-1]])
                        P.add("dve", lambda e: e.tensor_scalar_mul(row(3), pb[3][0:1, 0:FT], INV_D))
                        P.add("dve", lambda e: e.tensor_mul(row(2), row(0), row(0)))
                        P.add("dve", lambda e: e.tensor_sub(row(3), row(3), row(2)))
                        ev_var = P.add("dve", lambda e: e.tensor_scalar_add(
                            row(3), row(3), EPS_LN), inc="Dv")
                        ev_ln = P.add("act", lambda e: e.activation(row(2), row(3), AF.Ln),
                                      waits=[ev_var], inc="Ac")
                        ev_hf = P.add("dve", lambda e: e.tensor_scalar_mul(
                            row(2), row(2), -0.5), waits=[ev_ln], inc="Dv")
                        ev_rstd = P.add("act", lambda e: e.activation(row(1), row(2), AF.Exp),
                                        waits=[ev_hf], inc="Ac")
                        ev_b = P.add("dve", lambda e: e.scalar_tensor_tensor(
                            row(2), row(0), -1.0, row(1), ALU.mult, ALU.mult),
                            waits=[ev_rstd], inc="Dv")
                        eva = P.add("pe", lambda e: e.matmul(
                            pb[6][:, 0:FT], onec_s[:], row(1), start=True, stop=True),
                            waits=[ev_b, ev_flush] + war67, inc="Pe")
                        evb = P.add("pe", lambda e: e.matmul(
                            pb[7][:, 0:FT], onec_s[:], row(2), start=True, stop=True),
                            inc="Pe")
                        box.append(evb)
                    return em
                yield (1.0, (lambda: True), em_rows(rows_box, st_box, sl_pb67_war[0]))

                x1_box = []

                def em_x1(q, rows_box, x1b_war, box):
                    def em():
                        for hg in range(q * 4, q * 4 + 4):
                            P.add("dve", lambda e, hg=hg: e.tensor_mul(
                                yfv(hg), yfv(hg), pb[6][:, 0:FT]),
                                waits=[rows_box[0]])
                            P.add("dve", lambda e, hg=hg: e.tensor_add(
                                yfv(hg), yfv(hg), pb[7][:, 0:FT]))
                            ev = P.add("dve", lambda e, hg=hg: e.tensor_copy(
                                x1bv(hg), yfv(hg)),
                                waits=(x1b_war if hg == 0 else []), inc="Dv")
                            box.append(ev)
                    return em
                for q in range(4):
                    yield (4.0, (lambda: True), em_x1(q, rows_box, sl_x1b_war[0], x1_box))

                fdma_evs = [None] * NIT
                my_frd_prev = last_frd_ev[0]

                a67_box = []

                def em_A(it, x1_box, fdma_evs, my_frd_prev, a67_box=a67_box):
                    ii = it % 2
                    gb, ub = (2, 3) if ii == 0 else (6, 7)

                    def em():
                        evw = P.add("sync", lambda e, it=it, ii=ii: e.dma_start(
                            out=b16v(o_wgu[ii], 4096), in_=hgu_bf[it]),
                            waits=[("Gp", it + 1), wgu_war[ii]], inc=f"Gw{ii}")
                        evg = evu = None
                        for kg in range(DBA):
                            st, sp = kg == 0, kg == DBA - 1
                            evg = P.add("pe", lambda e, kg=kg, st=st, sp=sp, gb=gb, ii=ii: e.matmul(
                                pb[gb][:, 0:FT], wguv(ii, kg * 128, 128), x1bv(kg),
                                start=st, stop=sp),
                                waits=[evw, x1_box[-1]] + ([prev_gt[ii]] if st else []),
                                inc="Pe" if sp else None) or evg
                            evu = P.add("pe", lambda e, kg=kg, st=st, sp=sp, ub=ub, ii=ii: e.matmul(
                                pb[ub][:, 0:FT], wguv(ii, 2048 + kg * 128, 128), x1bv(kg),
                                start=st, stop=sp),
                                waits=([prev_ft[ii]] if st else []),
                                inc="Pe" if sp else None) or evu
                        wgu_war[ii] = evu
                        sl_x1b_war[0] = [evu]
                        evgt = P.add("act", lambda e, it=it, ii=ii, gb=gb: e.activation(
                            gtv(ii), pb[gb][:, 0:FT], AF.Silu, scale=sg_s[:, it:it + 1]),
                            waits=[evg, evu, gt_war[ii]], inc="Ac")
                        evft = P.add("dve", lambda e, it=it, ii=ii, ub=ub: e.scalar_tensor_tensor(
                            ftv(ii), pb[ub][:, 0:FT], su_s[:, it:it + 1], gtv(ii),
                            ALU.mult, ALU.mult),
                            waits=[evgt, ft_war[ii]], inc="Dv")
                        prev_gt[ii] = evgt
                        prev_ft[ii] = evft
                        gt_war[ii] = evft
                        evfd = P.add("sync", lambda e, it=it, ii=ii: e.dma_start(
                            out=fuse_d[it], in_=ftv(ii)),
                            waits=[evft, my_frd_prev[it]], inc=f"Lf{ii}")
                        fdma_evs[it] = evfd
                        ft_war[ii] = evfd
                        if it == NIT - 2:
                            sl_pb23_war[0] = [evgt, evft]
                        if it == NIT - 1:
                            a67_box.append(evgt)
                            a67_box.append(evft)
                    return em
                for it in range(NIT):
                    yield (4.0, (lambda: True), em_A(it, x1_box, fdma_evs, my_frd_prev))

                my_frd = [None] * NIT
                last_frd_ev[0] = my_frd
                evac_box = []

                def em_B(p, it0, fdma_evs, my_frd, evac_box):
                    def em():
                        for it in range(it0, it0 + 8):
                            ii = it % 2
                            evwd = P.add("sync", lambda e, it=it, ii=ii, p=p: e.dma_start(
                                out=b16v(o_wd[ii], 512),
                                in_=hd_bf[it, :, p * 512:(p + 1) * 512]),
                                waits=[("Gp", NIT + it + 1), wd_war[ii]], inc=f"Gv{ii}")
                            evfr = P.add("sync", lambda e, it=it, ii=ii: e.dma_start(
                                out=frdv(ii), in_=fuse_d[it]),
                                waits=[fdma_evs[it], frd_war[ii]], inc=f"Lr{ii}")
                            evd = None
                            for g in range(4):
                                st, sp = it == 0, it == NIT - 1
                                w0 = [evwd, evfr]
                                if st:
                                    w0 = w0 + [pbB_war[g]]
                                    if g >= 2:
                                        w0 = w0 + a67_box
                                evd = P.add("pe", lambda e, it=it, ii=ii, g=g, st=st, sp=sp: e.matmul(
                                    pb[4 + g][:, 0:FT], wdv(ii, g * 128, 128), frdv(ii),
                                    start=st, stop=sp), waits=w0,
                                    inc="Pe" if (g == 3 or sp) else None) or evd
                            wd_war[ii] = evd
                            frd_war[ii] = evd
                            my_frd[it] = evd
                            if it == NIT - 1:
                                for g in range(4):
                                    hg = p * 4 + g
                                    eve = P.add("dve", lambda e, g=g, hg=hg: e.scalar_tensor_tensor(
                                        yfv(hg), pb[4 + g][:, 0:FT], sd_s[:, hg:hg + 1],
                                        yfv(hg), ALU.mult, ALU.add),
                                        waits=[evd], inc="Dv")
                                    pbB_war[g] = eve
                                    evac_box.append(eve)
                    return em
                for p in range(4):
                    for it0 in range(0, NIT, 8):
                        yield (4.5, (lambda: True), em_B(p, it0, fdma_evs, my_frd, evac_box))

                st2_box = []
                pre2 = lambda hg: [evac_box[hg // 4 * 4 + hg % 4]]
                for q in range(4):
                    yield (1.5, (lambda: True),
                           em_stats(q, yfv, pre2, sl_pb23_war[0], st2_box))
                rows2_box = []
                yield (1.0, (lambda: True), em_rows(rows2_box, st2_box, [evac_box[-1]]))

                out_box = []

                def em_out(q, rows2_box, out_box, j=j):
                    def em():
                        for hg in range(q * 4, q * 4 + 4):
                            P.add("dve", lambda e, hg=hg: e.tensor_mul(
                                ostv(hg % 2), yfv(hg), pb[6][:, 0:FT]),
                                waits=[rows2_box[0], ost_war[hg % 2]])
                            evo = P.add("dve", lambda e, hg=hg: e.tensor_add(
                                ostv(hg % 2), ostv(hg % 2), pb[7][:, 0:FT]), inc="Dv")
                            evod = P.add("sync", lambda e, hg=hg, j=j: e.dma_start(
                                out=out[hg * 128:(hg + 1) * 128, j * FT:(j + 1) * FT],
                                in_=ostv(hg % 2)),
                                waits=[evo], inc=f"Lo{hg % 2}")
                            ost_war[hg % 2] = evod
                            out_box.append(evo)
                    return em
                for q in range(4):
                    yield (2.5, (lambda: True), em_out(q, rows2_box, out_box))

                def em_seal(out_box, st2_box):
                    def em():
                        sl_yf_war[0] = out_box[-4:]
                        sl_pb67_war[0] = out_box[-4:]
                        sl_pb23_war[0] = [st2_box[-1]]
                    return em
                yield (0.0, (lambda: True), em_seal(out_box, st2_box))

        class Puller:
            def __init__(self, gen):
                self.gen = gen
                self.pending = None
                self.done = False

            def pull(self, budget):
                while not self.done:
                    if self.pending is None:
                        try:
                            self.pending = next(self.gen)
                        except StopIteration:
                            self.done = True
                            break
                    cost, ready, emit = self.pending
                    if not ready():
                        break
                    if cost > budget:
                        break
                    emit()
                    budget -= cost
                    self.pending = None

            def drain(self):
                while not self.done:
                    self.pull(1e9)
                    if not self.done:
                        break

        puller = Puller(ffn_stream() if PART == "all" else iter(()))

        xt_war = [None] * 3
        acc_war = [None] * 2
        dA_war = [None] * 2
        u_war = [None] * 2
        dbx_war = [None] * 2
        h_war = [[], []]
        bsb_war = [None] * 2
        csb_war = [None] * 2
        psdt_war = [None, None]
        dt_ready = [None] * DBM
        y_init_ev = [None] * DBM
        chunk_bar = []
        prev_dtmm_last = None
        prev_bcd_last = None
        prev_abs_ev = None
        bcast_last = []
        ev_bcB = [None] * S
        ev_bcC = [None] * S

        for tau in range(NCH):
            t0 = tau * TC
            xc_ready = [None] * DBA
            for db in range(DBA):
                it4 = db % 3
                ld_ev = P.add("sync", lambda e, i=it4, r0=db * 128, c0=t0: e.dma_start(
                    out=xtv(i), in_=xT[r0:r0 + 128, c0:c0 + TC + 3]),
                    waits=[xt_war[it4]] + (chunk_bar if db < 4 else []), inc=f"Lx{it4}")
                i2 = db % 2
                P.add("dve", lambda e, i2=i2, i=it4, db=db: e.tensor_scalar(
                    accv(i2), xts(i, 0, TC), cw_s[:, db * 4:db * 4 + 1], None, ALU.mult),
                    waits=[ld_ev, acc_war[i2], evA] + (chunk_bar if db < 2 else []))
                P.add("dve", lambda e, i2=i2, i=it4, db=db: e.scalar_tensor_tensor(
                    acc2v(i2), xts(i, 1, TC), cw_s[:, db * 4 + 1:db * 4 + 2],
                    accv(i2), ALU.mult, ALU.add))
                P.add("dve", lambda e, i2=i2, i=it4, db=db: e.scalar_tensor_tensor(
                    accv(i2), xts(i, 2, TC), cw_s[:, db * 4 + 2:db * 4 + 3],
                    acc2v(i2), ALU.mult, ALU.add))
                tap_ev = P.add("dve", lambda e, i2=i2, i=it4, db=db: e.scalar_tensor_tensor(
                    acc2v(i2), xts(i, 3, TC), cw_s[:, db * 4 + 3:db * 4 + 4],
                    accv(i2), ALU.mult, ALU.add), inc="Dv")
                if db < DBM:
                    sl_ev = P.add("act", lambda e, i2=i2, db=db: e.activation(
                        xcm(db), acc2v(i2), AF.Silu, bias=cb_s[:, db:db + 1]),
                        waits=[tap_ev] + (chunk_bar if db < 2 else []), inc="Ac")
                    cp_ev = P.add("dve", lambda e, db=db: e.tensor_copy(xcb(db), xcm(db)),
                                  waits=[sl_ev], inc="Dv")
                    yi_ev = P.add("dve", lambda e, db=db, i=it4: e.scalar_tensor_tensor(
                        yv(db), xcm(db), dp_s[:, db:db + 1], xts(i, 3, TC),
                        ALU.mult, ALU.add), inc="Dv")
                    y_init_ev[db] = yi_ev
                    xc_ready[db] = cp_ev
                    xt_war[it4] = yi_ev
                    acc_war[i2] = cp_ev
                else:
                    sl_ev = P.add("act", lambda e, i2=i2, db=db: e.activation(
                        xcb(db), acc2v(i2), AF.Silu, bias=cb_s[:, db:db + 1]),
                        waits=[tap_ev], inc="Ac")
                    xc_ready[db] = sl_ev
                    xt_war[it4] = tap_ev
                    acc_war[i2] = sl_ev
                if db == 7:
                    pump_casts()
                    puller.pull(PACE)
            proj_ev = None
            for kg in range(DBA):
                st, sp = kg == 0, kg == DBA - 1
                P.add("pe", lambda e, kg=kg, st=st, sp=sp: e.matmul(
                    pb[0][:, :], wxp_s[:, kg * 160:kg * 160 + 128], xcb(kg), start=st, stop=sp),
                    waits=[xc_ready[kg], ev_wdt, ev_wxp]
                    + ([prev_dtmm_last] + chunk_bar if st else []),
                    inc="Pe" if sp else None)
                proj_ev = P.add("pe", lambda e, kg=kg, st=st, sp=sp: e.matmul(
                    pb[1][0:32, :], wxp_s[:, kg * 160 + 128:kg * 160 + 160], xcb(kg),
                    start=st, stop=sp), inc="Pe" if sp else None) or proj_ev
            ev_dtr = P.add("act", lambda e: e.copy(dtr_s[:], pb[0][:, :]),
                           waits=[proj_ev, prev_dtmm_last], inc="Ac")
            ev_bc = P.add("act", lambda e: e.copy(bc_s[:], pb[1][0:32, :]),
                          waits=[proj_ev, prev_bcd_last], inc="Ac")
            ev_bcd = P.add("sync", lambda e, tau=tau: e.dma_start(
                out=bcd[tau % 2].ap(), in_=bc_s[:]),
                waits=[ev_bc] + (bcast_last[tau - 2] if tau >= 2 else []), inc="Lbc")
            prev_bcd_last = ev_bcd
            for db in range(DBM):
                i2 = db % 2
                mm_ev = P.add("pe", lambda e, db=db, i2=i2: e.matmul(
                    pb[i2][:, :], wdt_s[:, db * 128:(db + 1) * 128], dtr_s[:],
                    start=True, stop=True),
                    waits=[ev_dtr, ev_bcd, psdt_war[i2]], inc="Pe")
                ev_usb = P.add("dve", lambda e, db=db, i2=i2: e.tensor_scalar_add(
                    usbv, pb[i2][:, :], dtb_s[:, db:db + 1]),
                    waits=[mm_ev, prev_abs_ev], inc="Dv")
                P.add("dve", lambda e: e.tensor_scalar_max(rlv, usbv, 0.0))
                ev_abs = P.add("act", lambda e: e.activation(sp0v, usbv, AF.Abs),
                               waits=[ev_usb], inc="Ac")
                P.add("act", lambda e: e.activation(sp1v, sp0v, AF.Exp, scale=-1.0))
                ab_ev = P.add("act", lambda e: e.activation(
                    sp0v, sp1v, AF.Ln, bias=1.0), inc="Ac")
                dt_ev = P.add("dve", lambda e, db=db: e.scalar_tensor_tensor(
                    dtv(db), rlv, 1.0, sp0v, ALU.mult, ALU.add),
                    waits=[ab_ev], inc="Dv")
                P.add("dve", lambda e, db=db: e.scalar_tensor_tensor(
                    wv_(db), dtv(db), 1.0, xcm(db), ALU.mult, ALU.mult), inc="Dv")
                dt_ready[db] = dt_ev
                prev_abs_ev = dt_ev
                psdt_war[i2] = ev_usb
            pump_casts()
            puller.pull(PACE)
            for s in range(2):
                ev_bcB[s] = P.add("act", lambda e, s=s, tau=tau: e.dma_start(
                    out=bsbv(s % 2),
                    in_=bcd[tau % 2].ap()[s:s + 1, :].broadcast_to([128, TC])),
                    waits=[ev_bcd, bsb_war[s % 2]], inc=f"Lb{s % 2}")
                ev_bcC[s] = P.add("act", lambda e, s=s, tau=tau: e.dma_start(
                    out=csbv(s % 2),
                    in_=bcd[tau % 2].ap()[16 + s:17 + s, :].broadcast_to([128, TC])),
                    waits=[ev_bcd, csb_war[s % 2]], inc=f"Lb{2 + s % 2}")
            scan_ev = None
            last_evx = None
            yadd_last = None
            NI = S * DBM
            dA_evs = [None] * NI
            u_evs = [None] * NI

            def emit_front(j, tau=tau):
                s, db = j // DBM, j % DBM
                jj = j % 2
                dA_evs[j] = P.add("act", lambda e, db=db, jj=jj, col=db * 16 + s: e.activation(
                    dAx(jj), dtv(db), AF.Exp, scale=A_s[:, col:col + 1]),
                    waits=[dt_ready[db], dA_war[jj], ev_st0, ev_ms[jj]], inc="Ac")
                ev_u = P.add("dve", lambda e, db=db, jj=jj, s2=s % 2: e.scalar_tensor_tensor(
                    uv(jj), wv_(db), 1.0, bsbv(s2), ALU.mult, ALU.mult),
                    waits=[ev_bcB[s], u_war[jj]], inc="Dv")
                P.add("act", lambda e, jj=jj: e.activation(
                    dbxx(jj), uv(jj), AF.Relu), waits=[ev_u, dbx_war[jj]])
                u_evs[j] = P.add("act", lambda e, jj=jj, i=j: e.copy(
                    dbx0(jj), state[:, i:i + 1]), inc="Ac")

            emit_front(0)
            for i in range(NI):
                s, db = i // DBM, i % DBM
                ii = i % 2
                if i + 1 < NI:
                    emit_front(i + 1)
                evh = P.add("dve", lambda e, ii=ii: e.tensor_tensor_scan(
                    hv(ii), dAv(ii), dbxv(ii), 0.0, ALU.mult, ALU.add),
                    waits=[dA_evs[i], u_evs[i]] + h_war[ii], inc="Dv")
                scan_ev = evh
                evx = P.add("act", lambda e, ii=ii, i=i: e.copy(
                    state[:, i:i + 1], hv(ii)[:, TC:TC + 1]), waits=[evh], inc="Ac")
                last_evx = evx
                if i % 2 == 0:
                    evv = P.add("dve", lambda e, ii=ii, s2=s % 2: e.scalar_tensor_tensor(
                        vv(ii), hx(ii), 1.0, csbv(s2), ALU.mult, ALU.mult),
                        waits=[ev_bcC[s]], inc="Dv")
                else:
                    evv = P.add("pool", lambda e, ii=ii, s2=s % 2: e.tensor_mul(
                        vv(ii), hx(ii), csbv(s2)),
                        waits=[evh, ev_bcC[s]], inc="Po")
                evy = P.add("pool", lambda e, ii=ii, db=db: e.tensor_add(
                    yv(db), yv(db), vv(ii)),
                    waits=[evv] + ([y_init_ev[db]] if s == 0 else []), inc="Po")
                yadd_last = evy
                dA_war[ii] = evy
                u_war[ii] = evh
                dbx_war[ii] = evh
                h_war[ii] = [evy, evx]
                if db == DBM - 1:
                    bsb_war[s % 2] = u_evs[i]
                    csb_war[s % 2] = evv
                    if s + 2 < S:
                        sn = s + 2
                        ev_bcB[sn] = P.add("act", lambda e, sn=sn, tau=tau: e.dma_start(
                            out=bsbv(sn % 2),
                            in_=bcd[tau % 2].ap()[sn:sn + 1, :].broadcast_to([128, TC])),
                            waits=[ev_bcd, bsb_war[sn % 2]], inc=f"Lb{sn % 2}")
                        ev_bcC[sn] = P.add("act", lambda e, sn=sn, tau=tau: e.dma_start(
                            out=csbv(sn % 2),
                            in_=bcd[tau % 2].ap()[16 + sn:17 + sn, :].broadcast_to([128, TC])),
                            waits=[ev_bcd, csb_war[sn % 2]], inc=f"Lb{2 + sn % 2}")
                    puller.pull(PACE)
            prev_dtmm_last = psdt_war[1]
            bcast_last.append([ev_bcB[14], ev_bcB[15], ev_bcC[14], ev_bcC[15]])
            yb_ev = P.add("sync", lambda e, tau=tau: e.dma_start(
                out=ybin[tau].ap().rearrange("(db p) t -> p db t", p=128),
                in_=a32[:, o_y:o_y + DBM * TC].rearrange("p (db t) -> p db t", db=DBM)),
                waits=[yadd_last], inc=f"Ly{tau}")
            ag_ev = P.add("pool", lambda e, tau=tau: e.collective_compute(
                "AllGather", ALU.bypass, replica_groups=PAIRS,
                ins=[ybin[tau].ap().opt()],
                outs=[ybag.ap()[tau * D:(tau + 1) * D, :].opt()]),
                waits=[yb_ev], inc="Cc")
            ag_evs.append(ag_ev)
            chunk_bar = [scan_ev, yadd_last, yb_ev, proj_ev, ev_dtr, ev_bc, last_evx]

        while not puller.done:
            puller.pull(1e9)

        if PART == "mixer":
            prev = ag_evs[-1]
            for q in range(2 * HG):
                eq = P.add("pool", lambda e, q=q: e.dma_start(
                    out=f32v(o_yf + (q % HG) * FT, FT),
                    in_=ybag.ap()[q * 128:(q + 1) * 128, 0:FT]),
                    waits=[prev], inc="Gf")
                prev = P.add("sync", lambda e, q=q: e.dma_start(
                    out=out[(q % HG) * 128:((q % HG) + 1) * 128,
                            (q // HG) * FT:(q // HG + 1) * FT],
                    in_=f32v(o_yf + (q % HG) * FT, FT)), waits=[eq], inc="Lo0")

        P.add("sync", lambda e: e.nop(),
              waits=[(s, P.cnt[s]) for s in list(P.cnt) if is_dma_sem(s)])

        with nc.Block() as block:
            @block.sync
            def _(eng):
                P.emit_engine("sync", eng, sems)

            @block.tensor
            def _(eng):
                P.emit_engine("pe", eng, sems)

            @block.scalar
            def _(eng):
                P.emit_engine("act", eng, sems)

            @block.vector
            def _(eng):
                P.emit_engine("dve", eng, sems)

            @block.gpsimd
            def _(eng):
                P.emit_engine("pool", eng, sems)

    return nc


_NC_CACHE = {}
_LAST_RAW = None


def _get_nc():
    if "nc" not in _NC_CACHE:
        _NC_CACHE["nc"] = build_nc()
    return _NC_CACHE["nc"]


def kernel(L0_attn_x, L0_attn_conv_w, L0_attn_conv_b, L0_attn_x_proj_w,
           L0_attn_dt_proj_w, L0_attn_dt_proj_b, L0_attn_A_log, L0_attn_D_param,
           L0_ffn_w_gate, L0_ffn_w_up, L0_ffn_w_down, L0_ffn_s_gate,
           L0_ffn_s_up, L0_ffn_s_down):
    x = np.asarray(L0_attn_x, np.float32)
    conv_w = np.asarray(L0_attn_conv_w, np.float32)
    conv_b = np.asarray(L0_attn_conv_b, np.float32)
    x_proj_w = np.asarray(L0_attn_x_proj_w, np.float32)
    dt_proj_w = np.asarray(L0_attn_dt_proj_w, np.float32)
    dt_proj_b = np.asarray(L0_attn_dt_proj_b, np.float32)
    A_log = np.asarray(L0_attn_A_log, np.float32)
    D_param = np.asarray(L0_attn_D_param, np.float32)
    wg8 = np.asarray(L0_ffn_w_gate)
    wu8 = np.asarray(L0_ffn_w_up)
    wd8 = np.asarray(L0_ffn_w_down)
    s_gate = np.asarray(L0_ffn_s_gate, np.float32)
    s_up = np.asarray(L0_ffn_s_up, np.float32)
    s_down = np.asarray(L0_ffn_s_down, np.float32)

    def half_prep(role):
        lo = role * 1024
        perm = np.concatenate([np.arange(lo, lo + 1024),
                               np.arange((1 - role) * 1024, (1 - role) * 1024 + 1024)])
        d = {}
        d["cw"] = np.ascontiguousarray(
            conv_w[perm, 0, :].reshape(16, 128, 4).transpose(1, 0, 2).reshape(128, 64))
        d["cbi"] = np.ascontiguousarray(conv_b[perm].reshape(16, 128).T)
        d["dpi"] = np.ascontiguousarray(D_param[lo:lo + 1024].reshape(8, 128).T)
        d["dtbi"] = np.ascontiguousarray(dt_proj_b[lo:lo + 1024].reshape(8, 128).T)
        d["alog"] = np.ascontiguousarray(
            A_log[lo:lo + 1024].reshape(8, 128, 16).transpose(1, 0, 2).reshape(128, 128))
        d["wxp"] = np.ascontiguousarray(x_proj_w.T[perm])
        d["wdt"] = np.ascontiguousarray(dt_proj_w[lo:lo + 1024].T)
        d["offs"] = np.full((1, 4), role * FT, np.uint32)
        return d, perm

    halves = [half_prep(0), half_prep(1)]

    onescol = np.ones((1, 128), np.float32)
    ones128 = np.ones((128, 1), np.float32)
    g4 = wg8.reshape(16, 128, 64, 128).transpose(2, 1, 0, 3)
    u4 = wu8.reshape(16, 128, 64, 128).transpose(2, 1, 0, 3)
    hgu = np.ascontiguousarray(np.stack([g4, u4], axis=2).reshape(NIT, 128, 4096))
    hd = np.ascontiguousarray(wd8.reshape(NIT, 128, 2048))
    sgl = np.ascontiguousarray(s_gate.reshape(64, 128).T)
    sul = np.ascontiguousarray(s_up.reshape(64, 128).T)
    sdl = np.ascontiguousarray(s_down.reshape(16, 128).T)

    in_maps = []
    for c in range(8):
        b, role = c // 2, c % 2
        d, perm = halves[role]
        xTh = np.zeros((D, T + 3), np.float32)
        xTh[:, 3:] = x[b].T[perm]
        m = {"xT": xTh, "onescol": onescol, "ones128": ones128,
             "hgu": hgu, "hd": hd, "sgd": sgl, "sud": sul, "sdd": sdl}
        m.update(d)
        in_maps.append(m)

    nc = _get_nc()
    res = run_bass_kernel_spmd(nc, in_maps, core_ids=list(range(8)))

    global _LAST_RAW
    _LAST_RAW = [np.asarray(res.results[c]["out"]) for c in range(8)]
    result = np.empty((BATCH, T, D), np.float32)
    for c in range(8):
        b, role = c // 2, c % 2
        o = np.asarray(res.results[c]["out"])
        for j in range(NCH):
            result[b, j * TC + role * FT: j * TC + (role + 1) * FT, :] = \
                o[:, j * FT:(j + 1) * FT].T
    return result


# version 16
# speedup vs baseline: 1.2528x; 1.2482x over previous
import contextlib
import numpy as np
import ml_dtypes
import concourse.bass as bass
import concourse.mybir as mybir
from concourse.bass_utils import run_bass_kernel_spmd

F32 = mybir.dt.float32
BF16 = mybir.dt.bfloat16
I8 = mybir.dt.int8
U32 = mybir.dt.uint32
AF = mybir.ActivationFunctionType
ALU = mybir.AluOpType

D = 2048
T = 2048
BATCH = 4
TC = 512
NCH = T // TC
FT = 256
DBA = 16
DBM = 8
S = 16
RK = 128
NIT = 64
HG = 16
INV_D = 1.0 / D
EPS_LN = 1e-5
PAIRS = [[0, 1], [2, 3], [4, 5], [6, 7]]
PACE = 0.0

ENGINES = ("sync", "pe", "act", "dve", "pool")
OWN_SEM = {"pe": "Pe", "act": "Ac", "dve": "Dv", "pool": "Po"}
def is_dma_sem(s):
    return s[0] in ("L", "G")


class Prog:

    def __init__(self):
        self.ops = []
        self.cnt = {}

    def ev_last(self, sem):
        return (sem, self.cnt.get(sem, 0))

    def add(self, eng, fn, waits=(), inc=None):
        ev = None
        if inc is not None:
            self.cnt[inc] = self.cnt.get(inc, 0) + 1
            ev = (inc, self.cnt[inc])
        self.ops.append((eng, fn, tuple(w for w in waits if w is not None), inc))
        return ev

    def emit_engine(self, name, eng, sems):
        known = {}
        own = OWN_SEM.get(name)
        for oeng, fn, waits, inc in self.ops:
            if oeng != name:
                continue
            for sem, val in waits:
                if sem.endswith("!"):
                    sem = sem[:-1]
                elif sem == own:
                    continue
                tgt = val * 16 if is_dma_sem(sem) else val
                if known.get(sem, 0) < tgt:
                    eng.wait_ge(sems[sem], tgt)
                    known[sem] = tgt
            ins = fn(eng)
            if inc is not None:
                ins.then_inc(sems[inc], 16 if is_dma_sem(inc) else 1)


PART = "all"


def build_nc():
    nc = bass.Bass("TRN2", target_bir_lowering=False, debug=False)

    xT = nc.dram_tensor("xT", [D, T + 3], F32, kind="ExternalInput").ap()
    cw = nc.dram_tensor("cw", [128, 64], F32, kind="ExternalInput").ap()
    cbi = nc.dram_tensor("cbi", [128, 16], F32, kind="ExternalInput").ap()
    dpi = nc.dram_tensor("dpi", [128, 8], F32, kind="ExternalInput").ap()
    dtbi = nc.dram_tensor("dtbi", [128, 8], F32, kind="ExternalInput").ap()
    alog = nc.dram_tensor("alog", [128, 128], F32, kind="ExternalInput").ap()
    wxp = nc.dram_tensor("wxp", [D, 160], F32, kind="ExternalInput").ap()
    wdt = nc.dram_tensor("wdt", [RK, 1024], F32, kind="ExternalInput").ap()
    onescol = nc.dram_tensor("onescol", [1, 128], F32, kind="ExternalInput").ap()
    ones128 = nc.dram_tensor("ones128", [128, 1], F32, kind="ExternalInput").ap()
    hgu = nc.dram_tensor("hgu", [NIT, 128, 4096], I8, kind="ExternalInput").ap()
    hd = nc.dram_tensor("hd", [NIT, 128, 2048], I8, kind="ExternalInput").ap()
    sgd = nc.dram_tensor("sgd", [128, 64], F32, kind="ExternalInput").ap()
    sud = nc.dram_tensor("sud", [128, 64], F32, kind="ExternalInput").ap()
    sdd = nc.dram_tensor("sdd", [128, 16], F32, kind="ExternalInput").ap()
    offs = nc.dram_tensor("offs", [1, 4], U32, kind="ExternalInput").ap()

    out = nc.dram_tensor("out", [D, NCH * FT], F32, kind="ExternalOutput").ap()

    ybin = [nc.dram_tensor(f"ybin{t}", [1024, TC], F32) for t in range(NCH)]
    ybag = nc.dram_tensor("ybag", [NCH * D, TC], F32)
    bcd = [nc.dram_tensor(f"bcd{i}", [32, TC], F32) for i in range(2)]
    fuse_d = nc.dram_tensor("fuse_d", [NIT, 128, FT], BF16)

    ctx = contextlib.ExitStack()
    sb = lambda name, shape, dt: ctx.enter_context(nc.sbuf_tensor(name, shape, dt))
    psb = lambda name, shape: ctx.enter_context(nc.psum_tensor(name, shape, F32))

    with ctx:
        cw_s = sb("cw_s", [128, 64], F32)
        cb_s = sb("cb_s", [128, 16], F32)
        dp_s = sb("dp_s", [128, 8], F32)
        dtb_s = sb("dtb_s", [128, 8], F32)
        A_s = sb("A_s", [128, 128], F32)
        Atmp_s = sb("Atmp_s", [128, 128], F32)
        wxp_s = sb("wxp_s", [128, 16 * 160], BF16)
        wdt_s = sb("wdt_s", [128, 1024], BF16)
        onec_s = sb("onec_s", [1, 128], F32)
        one128_s = sb("one128_s", [128, 1], F32)
        sg_s = sb("sg_s", [128, 64], F32)
        su_s = sb("su_s", [128, 64], F32)
        sd_s = sb("sd_s", [128, 16], F32)
        offs_s = sb("offs_s", [1, 4], U32)
        state = sb("state", [128, 128], F32)
        dtr_s = sb("dtr_s", [128, TC], BF16)
        bc_s = sb("bc_s", [32, TC], F32)

        AB = 185 * 1024
        arena = sb("arena", [128, AB], I8)
        a32 = arena.bitcast(F32)
        a16 = arena.bitcast(BF16)

        class Alloc:
            def __init__(self):
                self.cur = 0

            def f32(self, nelem):
                off = (self.cur + 63) // 64 * 64
                self.cur = off + nelem * 4
                assert self.cur <= AB, f"arena overflow {self.cur}"
                return off // 4

            def bf16(self, nelem):
                off = (self.cur + 63) // 64 * 64
                self.cur = off + nelem * 2
                assert self.cur <= AB, f"arena overflow {self.cur}"
                return off // 2

        M = Alloc()
        o_xt = [M.f32(TC + 3) for _ in range(3)]
        o_acc = [M.f32(TC) for _ in range(2)]
        o_acc2 = [M.f32(TC) for _ in range(2)]
        o_xcm = M.f32(DBM * TC)
        o_xcb = M.bf16(DBA * TC)
        o_dt = M.f32(DBM * TC)
        o_w = M.f32(DBM * TC)
        o_y = M.f32(DBM * TC)
        o_dA = [M.f32(TC + 16) for _ in range(2)]
        o_u = [M.f32(TC) for _ in range(2)]
        o_dbx = [M.f32(TC + 16) for _ in range(2)]
        o_h = [M.f32(TC + 16) for _ in range(2)]
        o_v = [M.f32(TC) for _ in range(2)]
        o_bsb = [M.f32(TC) for _ in range(2)]
        o_csb = [M.f32(TC) for _ in range(2)]
        o_usb = M.f32(TC)
        o_rl = M.f32(TC)
        o_sp0 = M.f32(TC)
        o_sp1 = M.f32(TC)
        o_yf = M.f32(HG * FT)
        o_x1b = M.bf16(HG * FT)
        o_wgu = [M.bf16(4096) for _ in range(2)]
        o_wd = [M.bf16(512) for _ in range(2)]
        o_frd = [M.bf16(FT) for _ in range(2)]
        o_gt = [M.f32(FT) for _ in range(2)]
        o_ft = [M.bf16(FT) for _ in range(2)]
        o_sq = [M.f32(FT) for _ in range(2)]
        o_rows = M.f32(4 * FT)
        o_ost = [M.f32(FT) for _ in range(2)]

        f32v = lambda off, n: a32[0:128, off:off + n]
        b16v = lambda off, n: a16[0:128, off:off + n]

        xtv = lambda i: f32v(o_xt[i], TC + 3)
        xts = lambda i, a, n: f32v(o_xt[i] + a, n)
        accv = lambda i: f32v(o_acc[i], TC)
        acc2v = lambda i: f32v(o_acc2[i], TC)
        xcm = lambda db: f32v(o_xcm + db * TC, TC)
        xcb = lambda db: b16v(o_xcb + db * TC, TC)
        dtv = lambda db: f32v(o_dt + db * TC, TC)
        wv_ = lambda db: f32v(o_w + db * TC, TC)
        yv = lambda db: f32v(o_y + db * TC, TC)
        dAv = lambda i: f32v(o_dA[i], TC + 1)
        dAx = lambda i: f32v(o_dA[i] + 1, TC)
        uv = lambda i: f32v(o_u[i], TC)
        dbxv = lambda i: f32v(o_dbx[i], TC + 1)
        dbxx = lambda i: f32v(o_dbx[i] + 1, TC)
        dbx0 = lambda i: f32v(o_dbx[i], 1)
        hv = lambda i: f32v(o_h[i], TC + 1)
        hx = lambda i: f32v(o_h[i] + 1, TC)
        vv = lambda i: f32v(o_v[i], TC)
        bsbv = lambda i: f32v(o_bsb[i], TC)
        csbv = lambda i: f32v(o_csb[i], TC)
        usbv = f32v(o_usb, TC)
        rlv = f32v(o_rl, TC)
        sp0v = f32v(o_sp0, TC)
        sp1v = f32v(o_sp1, TC)

        yfv = lambda hg: f32v(o_yf + hg * FT, FT)
        x1bv = lambda hg: b16v(o_x1b + hg * FT, FT)
        wguv = lambda i, a, n: b16v(o_wgu[i] + a, n)
        wdv = lambda i, a, n: b16v(o_wd[i] + a, n)
        frdv = lambda i: b16v(o_frd[i], FT)
        gtv = lambda i: f32v(o_gt[i], FT)
        ftv = lambda i: b16v(o_ft[i], FT)
        sqv = lambda i: f32v(o_sq[i], FT)
        row = lambda i: a32[0:1, o_rows + i * FT: o_rows + (i + 1) * FT]
        ostv = lambda i: f32v(o_ost[i], FT)

        pb = [psb(f"pb{i}", [128, 512]) for i in range(8)]

        sem_names = (["Lc", "Gc", "Lbc", "Cc", "Gf", "Pe", "Ac", "Dv", "Po"]
                     + [f"Lb{i}" for i in range(4)]
                     + [f"Lx{i}" for i in range(3)] + [f"Ly{i}" for i in range(NCH)]
                     + [f"Gw{i}" for i in range(2)] + [f"Gv{i}" for i in range(2)]
                     + [f"Lr{i}" for i in range(2)] + [f"Lf{i}" for i in range(2)]
                     + [f"Lo{i}" for i in range(2)])
        sems = {n: ctx.enter_context(nc.semaphore(n)) for n in sem_names}

        P = Prog()

        const_loads = [("cw", cw_s, cw), ("cb", cb_s, cbi), ("dp", dp_s, dpi),
                       ("dtb", dtb_s, dtbi), ("alog", Atmp_s, alog),
                       ("onec", onec_s, onescol),
                       ("one128", one128_s, ones128), ("sg", sg_s, sgd),
                       ("su", su_s, sud), ("sd", sd_s, sdd), ("offs", offs_s, offs)]
        const_evs = {}
        for name, dst, src in const_loads:
            const_evs[name] = P.add("sync", lambda e, d=dst, s=src: e.dma_start(out=d[:], in_=s), inc="Lc")
        ev_wxp = P.add("pool", lambda e: e.dma_start(
            out=wxp_s.rearrange("p (kg f) -> p kg f", kg=16),
            in_=wxp.rearrange("(kg p) f -> p kg f", p=128)), inc="Gc")
        ev_wdt = P.add("pool", lambda e: e.dma_start(out=wdt_s[:], in_=wdt), inc="Gc")
        ev_flush = P.add("sync", lambda e: e.dma_start(out=Atmp_s[:], in_=alog), inc="Lc")
        evA1 = P.add("act", lambda e: e.activation(A_s[:], Atmp_s[:], AF.Exp),
                     waits=[ev_flush, const_evs["alog"]], inc="Ac")
        evA = P.add("dve", lambda e: e.tensor_scalar_mul(A_s[:], A_s[:], -1.0),
                    waits=[evA1], inc="Dv")
        ev_st0 = P.add("dve", lambda e: e.memset(state[:], 0.0), inc="Dv")
        ev_ms = []
        for i in range(2):
            ev_ms.append(P.add("dve", lambda e, i=i: e.memset(
                f32v(o_dA[i], 1), 0.0), inc="Dv"))

        ag_evs = []
        sl_yf_war = [[]]
        sl_pb23_war = [[]]
        sl_pb67_war = [[]]
        sl_x1b_war = [[]]
        pbB_war = [None] * 4
        last_frd_ev = [[None] * NIT]

        def ffn_stream():
            wgu_war = [None, None]
            prev_gt = [None, None]
            prev_ft = [None, None]
            gt_war = [None, None]
            ft_war = [None, None]
            wd_war = [None, None]
            frd_war = [None, None]
            ost_war = [None, None]
            for j in range(NCH):
                yf_ev_box = []

                def em_yf(j=j):
                    def ld(e, j=j):
                        v = e.value_load(offs_s[0:1, j:j + 1])
                        src = ybag.ap()[j * D:(j + 1) * D, bass.ds(v, FT)] \
                            .rearrange("(hg p) t -> p hg t", p=128)
                        return e.dma_start(
                            out=a32[0:128, o_yf:o_yf + HG * FT].rearrange(
                                "p (hg t) -> p hg t", hg=HG), in_=src)
                    ev = P.add("sync", ld,
                               waits=[("Cc", j + 1)] + sl_yf_war[0], inc="Gf")
                    yf_ev_box.append(ev)
                yield (1.0, (lambda j=j: len(ag_evs) > j), em_yf)

                st_box = []

                def em_stats(q, src_v, pre, war23, box):
                    def em():
                        for hg in range(q * 4, q * 4 + 4):
                            evq = P.add("act", lambda e, hg=hg: e.activation(
                                sqv(hg % 2), src_v(hg), AF.Square),
                                waits=pre(hg) + [box[-1] if (hg >= 2 and box) else None],
                                inc="Ac")
                            P.add("pe", lambda e, hg=hg: e.matmul(
                                pb[2][0:1, 0:FT], one128_s[:], src_v(hg),
                                start=(hg == 0), stop=(hg == HG - 1)),
                                waits=pre(hg) + [ev_flush] + (war23 if hg == 0 else []))
                            ev2 = P.add("pe", lambda e, hg=hg: e.matmul(
                                pb[3][0:1, 0:FT], one128_s[:], sqv(hg % 2),
                                start=(hg == 0), stop=(hg == HG - 1)),
                                waits=[evq], inc="Pe")
                            box.append(ev2)
                    return em
                pre1 = lambda hg: [yf_ev_box[0]]
                for q in range(4):
                    yield (1.5, (lambda: True),
                           em_stats(q, yfv, pre1, sl_pb23_war[0], st_box))

                rows_box = []

                def em_rows(box, st_box, war67):
                    def em():
                        ev0 = P.add("dve", lambda e: e.tensor_scalar_mul(
                            row(0), pb[2][0:1, 0:FT], INV_D), waits=[st_box[-1]])
                        P.add("dve", lambda e: e.tensor_scalar_mul(row(3), pb[3][0:1, 0:FT], INV_D))
                        P.add("dve", lambda e: e.tensor_mul(row(2), row(0), row(0)))
                        P.add("dve", lambda e: e.tensor_sub(row(3), row(3), row(2)))
                        ev_var = P.add("dve", lambda e: e.tensor_scalar_add(
                            row(3), row(3), EPS_LN), inc="Dv")
                        ev_ln = P.add("act", lambda e: e.activation(row(2), row(3), AF.Ln),
                                      waits=[ev_var], inc="Ac")
                        ev_hf = P.add("dve", lambda e: e.tensor_scalar_mul(
                            row(2), row(2), -0.5), waits=[ev_ln], inc="Dv")
                        ev_rstd = P.add("act", lambda e: e.activation(row(1), row(2), AF.Exp),
                                        waits=[ev_hf], inc="Ac")
                        ev_b = P.add("dve", lambda e: e.scalar_tensor_tensor(
                            row(2), row(0), -1.0, row(1), ALU.mult, ALU.mult),
                            waits=[ev_rstd], inc="Dv")
                        eva = P.add("pe", lambda e: e.matmul(
                            pb[6][:, 0:FT], onec_s[:], row(1), start=True, stop=True),
                            waits=[ev_b, ev_flush] + war67, inc="Pe")
                        evb = P.add("pe", lambda e: e.matmul(
                            pb[7][:, 0:FT], onec_s[:], row(2), start=True, stop=True),
                            inc="Pe")
                        box.append(evb)
                    return em
                yield (1.0, (lambda: True), em_rows(rows_box, st_box, sl_pb67_war[0]))

                x1_box = []

                def em_x1(q, rows_box, x1b_war, box):
                    def em():
                        for hg in range(q * 4, q * 4 + 4):
                            P.add("dve", lambda e, hg=hg: e.tensor_mul(
                                yfv(hg), yfv(hg), pb[6][:, 0:FT]),
                                waits=[rows_box[0]])
                            P.add("dve", lambda e, hg=hg: e.tensor_add(
                                yfv(hg), yfv(hg), pb[7][:, 0:FT]))
                            ev = P.add("dve", lambda e, hg=hg: e.tensor_copy(
                                x1bv(hg), yfv(hg)),
                                waits=(x1b_war if hg == 0 else []), inc="Dv")
                            box.append(ev)
                    return em
                for q in range(4):
                    yield (4.0, (lambda: True), em_x1(q, rows_box, sl_x1b_war[0], x1_box))

                fdma_evs = [None] * NIT
                my_frd_prev = last_frd_ev[0]

                a67_box = []

                def em_A(it, x1_box, fdma_evs, my_frd_prev, a67_box=a67_box):
                    ii = it % 2
                    gb, ub = (2, 3) if ii == 0 else (6, 7)

                    def em():
                        evw = P.add("pool", lambda e, it=it, ii=ii: e.dma_start(
                            out=b16v(o_wgu[ii], 4096), in_=hgu[it]),
                            waits=[wgu_war[ii]], inc=f"Gw{ii}")
                        evg = evu = None
                        for kg in range(DBA):
                            st, sp = kg == 0, kg == DBA - 1
                            evg = P.add("pe", lambda e, kg=kg, st=st, sp=sp, gb=gb, ii=ii: e.matmul(
                                pb[gb][:, 0:FT], wguv(ii, kg * 128, 128), x1bv(kg),
                                start=st, stop=sp),
                                waits=[evw, x1_box[-1]] + ([prev_gt[ii]] if st else []),
                                inc="Pe" if sp else None) or evg
                            evu = P.add("pe", lambda e, kg=kg, st=st, sp=sp, ub=ub, ii=ii: e.matmul(
                                pb[ub][:, 0:FT], wguv(ii, 2048 + kg * 128, 128), x1bv(kg),
                                start=st, stop=sp),
                                waits=([prev_ft[ii]] if st else []),
                                inc="Pe" if sp else None) or evu
                        wgu_war[ii] = evu
                        sl_x1b_war[0] = [evu]
                        evgt = P.add("act", lambda e, it=it, ii=ii, gb=gb: e.activation(
                            gtv(ii), pb[gb][:, 0:FT], AF.Silu, scale=sg_s[:, it:it + 1]),
                            waits=[evg, evu, gt_war[ii]], inc="Ac")
                        evft = P.add("dve", lambda e, it=it, ii=ii, ub=ub: e.scalar_tensor_tensor(
                            ftv(ii), pb[ub][:, 0:FT], su_s[:, it:it + 1], gtv(ii),
                            ALU.mult, ALU.mult),
                            waits=[evgt, ft_war[ii]], inc="Dv")
                        prev_gt[ii] = evgt
                        prev_ft[ii] = evft
                        gt_war[ii] = evft
                        evfd = P.add("sync", lambda e, it=it, ii=ii: e.dma_start(
                            out=fuse_d[it], in_=ftv(ii)),
                            waits=[evft, my_frd_prev[it]], inc=f"Lf{ii}")
                        fdma_evs[it] = evfd
                        ft_war[ii] = evfd
                        if it == NIT - 2:
                            sl_pb23_war[0] = [evgt, evft]
                        if it == NIT - 1:
                            a67_box.append(evgt)
                            a67_box.append(evft)
                    return em
                for it in range(NIT):
                    yield (4.0, (lambda: True), em_A(it, x1_box, fdma_evs, my_frd_prev))

                my_frd = [None] * NIT
                last_frd_ev[0] = my_frd
                evac_box = []

                def em_B(p, it0, fdma_evs, my_frd, evac_box):
                    def em():
                        for it in range(it0, it0 + 8):
                            ii = it % 2
                            evwd = P.add("pool", lambda e, it=it, ii=ii, p=p: e.dma_start(
                                out=b16v(o_wd[ii], 512),
                                in_=hd[it, :, p * 512:(p + 1) * 512]),
                                waits=[wd_war[ii]], inc=f"Gv{ii}")
                            evfr = P.add("sync", lambda e, it=it, ii=ii: e.dma_start(
                                out=frdv(ii), in_=fuse_d[it]),
                                waits=[fdma_evs[it], frd_war[ii]], inc=f"Lr{ii}")
                            evd = None
                            for g in range(4):
                                st, sp = it == 0, it == NIT - 1
                                w0 = [evwd, evfr]
                                if st:
                                    w0 = w0 + [pbB_war[g]]
                                    if g >= 2:
                                        w0 = w0 + a67_box
                                evd = P.add("pe", lambda e, it=it, ii=ii, g=g, st=st, sp=sp: e.matmul(
                                    pb[4 + g][:, 0:FT], wdv(ii, g * 128, 128), frdv(ii),
                                    start=st, stop=sp), waits=w0,
                                    inc="Pe" if (g == 3 or sp) else None) or evd
                            wd_war[ii] = evd
                            frd_war[ii] = evd
                            my_frd[it] = evd
                            if it == NIT - 1:
                                for g in range(4):
                                    hg = p * 4 + g
                                    eve = P.add("dve", lambda e, g=g, hg=hg: e.scalar_tensor_tensor(
                                        yfv(hg), pb[4 + g][:, 0:FT], sd_s[:, hg:hg + 1],
                                        yfv(hg), ALU.mult, ALU.add),
                                        waits=[evd], inc="Dv")
                                    pbB_war[g] = eve
                                    evac_box.append(eve)
                    return em
                for p in range(4):
                    for it0 in range(0, NIT, 8):
                        yield (4.5, (lambda: True), em_B(p, it0, fdma_evs, my_frd, evac_box))

                st2_box = []
                pre2 = lambda hg: [evac_box[hg // 4 * 4 + hg % 4]]
                for q in range(4):
                    yield (1.5, (lambda: True),
                           em_stats(q, yfv, pre2, sl_pb23_war[0], st2_box))
                rows2_box = []
                yield (1.0, (lambda: True), em_rows(rows2_box, st2_box, [evac_box[-1]]))

                out_box = []

                def em_out(q, rows2_box, out_box, j=j):
                    def em():
                        for hg in range(q * 4, q * 4 + 4):
                            P.add("dve", lambda e, hg=hg: e.tensor_mul(
                                ostv(hg % 2), yfv(hg), pb[6][:, 0:FT]),
                                waits=[rows2_box[0], ost_war[hg % 2]])
                            evo = P.add("dve", lambda e, hg=hg: e.tensor_add(
                                ostv(hg % 2), ostv(hg % 2), pb[7][:, 0:FT]), inc="Dv")
                            evod = P.add("sync", lambda e, hg=hg, j=j: e.dma_start(
                                out=out[hg * 128:(hg + 1) * 128, j * FT:(j + 1) * FT],
                                in_=ostv(hg % 2)),
                                waits=[evo], inc=f"Lo{hg % 2}")
                            ost_war[hg % 2] = evod
                            out_box.append(evo)
                    return em
                for q in range(4):
                    yield (2.5, (lambda: True), em_out(q, rows2_box, out_box))

                def em_seal(out_box, st2_box):
                    def em():
                        sl_yf_war[0] = out_box[-4:]
                        sl_pb67_war[0] = out_box[-4:]
                        sl_pb23_war[0] = [st2_box[-1]]
                    return em
                yield (0.0, (lambda: True), em_seal(out_box, st2_box))

        class Puller:
            def __init__(self, gen):
                self.gen = gen
                self.pending = None
                self.done = False

            def pull(self, budget):
                while not self.done:
                    if self.pending is None:
                        try:
                            self.pending = next(self.gen)
                        except StopIteration:
                            self.done = True
                            break
                    cost, ready, emit = self.pending
                    if not ready():
                        break
                    if cost > budget:
                        break
                    emit()
                    budget -= cost
                    self.pending = None

            def drain(self):
                while not self.done:
                    self.pull(1e9)
                    if not self.done:
                        break

        puller = Puller(ffn_stream() if PART == "all" else iter(()))

        xt_war = [None] * 3
        acc_war = [None] * 2
        dA_war = [None] * 2
        u_war = [None] * 2
        dbx_war = [None] * 2
        h_war = [[], []]
        bsb_war = [None] * 2
        csb_war = [None] * 2
        psdt_war = [None, None]
        dt_ready = [None] * DBM
        y_init_ev = [None] * DBM
        chunk_bar = []
        prev_dtmm_last = None
        prev_bcd_last = None
        prev_abs_ev = None
        bcast_last = []
        ev_bcB = [None] * S
        ev_bcC = [None] * S

        for tau in range(NCH):
            t0 = tau * TC
            xc_ready = [None] * DBA
            for db in range(DBA):
                it4 = db % 3
                ld_ev = P.add("sync", lambda e, i=it4, r0=db * 128, c0=t0: e.dma_start(
                    out=xtv(i), in_=xT[r0:r0 + 128, c0:c0 + TC + 3]),
                    waits=[xt_war[it4]] + (chunk_bar if db < 4 else []), inc=f"Lx{it4}")
                i2 = db % 2
                P.add("dve", lambda e, i2=i2, i=it4, db=db: e.tensor_scalar(
                    accv(i2), xts(i, 0, TC), cw_s[:, db * 4:db * 4 + 1], None, ALU.mult),
                    waits=[ld_ev, acc_war[i2], evA] + (chunk_bar if db < 2 else []))
                P.add("dve", lambda e, i2=i2, i=it4, db=db: e.scalar_tensor_tensor(
                    acc2v(i2), xts(i, 1, TC), cw_s[:, db * 4 + 1:db * 4 + 2],
                    accv(i2), ALU.mult, ALU.add))
                P.add("dve", lambda e, i2=i2, i=it4, db=db: e.scalar_tensor_tensor(
                    accv(i2), xts(i, 2, TC), cw_s[:, db * 4 + 2:db * 4 + 3],
                    acc2v(i2), ALU.mult, ALU.add))
                tap_ev = P.add("dve", lambda e, i2=i2, i=it4, db=db: e.scalar_tensor_tensor(
                    acc2v(i2), xts(i, 3, TC), cw_s[:, db * 4 + 3:db * 4 + 4],
                    accv(i2), ALU.mult, ALU.add), inc="Dv")
                if db < DBM:
                    sl_ev = P.add("act", lambda e, i2=i2, db=db: e.activation(
                        xcm(db), acc2v(i2), AF.Silu, bias=cb_s[:, db:db + 1]),
                        waits=[tap_ev] + (chunk_bar if db < 2 else []), inc="Ac")
                    cp_ev = P.add("dve", lambda e, db=db: e.tensor_copy(xcb(db), xcm(db)),
                                  waits=[sl_ev], inc="Dv")
                    yi_ev = P.add("dve", lambda e, db=db, i=it4: e.scalar_tensor_tensor(
                        yv(db), xcm(db), dp_s[:, db:db + 1], xts(i, 3, TC),
                        ALU.mult, ALU.add), inc="Dv")
                    y_init_ev[db] = yi_ev
                    xc_ready[db] = cp_ev
                    xt_war[it4] = yi_ev
                    acc_war[i2] = cp_ev
                else:
                    sl_ev = P.add("act", lambda e, i2=i2, db=db: e.activation(
                        xcb(db), acc2v(i2), AF.Silu, bias=cb_s[:, db:db + 1]),
                        waits=[tap_ev], inc="Ac")
                    xc_ready[db] = sl_ev
                    xt_war[it4] = tap_ev
                    acc_war[i2] = sl_ev
                if db == 7:
                    puller.pull(PACE)
            proj_ev = None
            for kg in range(DBA):
                st, sp = kg == 0, kg == DBA - 1
                P.add("pe", lambda e, kg=kg, st=st, sp=sp: e.matmul(
                    pb[0][:, :], wxp_s[:, kg * 160:kg * 160 + 128], xcb(kg), start=st, stop=sp),
                    waits=[xc_ready[kg], ev_wdt, ev_wxp]
                    + ([prev_dtmm_last] + chunk_bar if st else []),
                    inc="Pe" if sp else None)
                proj_ev = P.add("pe", lambda e, kg=kg, st=st, sp=sp: e.matmul(
                    pb[1][0:32, :], wxp_s[:, kg * 160 + 128:kg * 160 + 160], xcb(kg),
                    start=st, stop=sp), inc="Pe" if sp else None) or proj_ev
            ev_dtr = P.add("act", lambda e: e.copy(dtr_s[:], pb[0][:, :]),
                           waits=[proj_ev, prev_dtmm_last], inc="Ac")
            ev_bc = P.add("act", lambda e: e.copy(bc_s[:], pb[1][0:32, :]),
                          waits=[proj_ev, prev_bcd_last], inc="Ac")
            ev_bcd = P.add("sync", lambda e, tau=tau: e.dma_start(
                out=bcd[tau % 2].ap(), in_=bc_s[:]),
                waits=[ev_bc] + (bcast_last[tau - 2] if tau >= 2 else []), inc="Lbc")
            prev_bcd_last = ev_bcd
            for db in range(DBM):
                i2 = db % 2
                mm_ev = P.add("pe", lambda e, db=db, i2=i2: e.matmul(
                    pb[i2][:, :], wdt_s[:, db * 128:(db + 1) * 128], dtr_s[:],
                    start=True, stop=True),
                    waits=[ev_dtr, ev_bcd, psdt_war[i2]], inc="Pe")
                ev_usb = P.add("dve", lambda e, db=db, i2=i2: e.tensor_scalar_add(
                    usbv, pb[i2][:, :], dtb_s[:, db:db + 1]),
                    waits=[mm_ev, prev_abs_ev], inc="Dv")
                P.add("dve", lambda e: e.tensor_scalar_max(rlv, usbv, 0.0))
                ev_abs = P.add("act", lambda e: e.activation(sp0v, usbv, AF.Abs),
                               waits=[ev_usb], inc="Ac")
                P.add("act", lambda e: e.activation(sp1v, sp0v, AF.Exp, scale=-1.0))
                ab_ev = P.add("act", lambda e: e.activation(
                    sp0v, sp1v, AF.Ln, bias=1.0), inc="Ac")
                dt_ev = P.add("dve", lambda e, db=db: e.scalar_tensor_tensor(
                    dtv(db), rlv, 1.0, sp0v, ALU.mult, ALU.add),
                    waits=[ab_ev], inc="Dv")
                P.add("dve", lambda e, db=db: e.scalar_tensor_tensor(
                    wv_(db), dtv(db), 1.0, xcm(db), ALU.mult, ALU.mult), inc="Dv")
                dt_ready[db] = dt_ev
                prev_abs_ev = dt_ev
                psdt_war[i2] = ev_usb
            puller.pull(PACE)
            for s in range(2):
                ev_bcB[s] = P.add("act", lambda e, s=s, tau=tau: e.dma_start(
                    out=bsbv(s % 2),
                    in_=bcd[tau % 2].ap()[s:s + 1, :].broadcast_to([128, TC])),
                    waits=[ev_bcd, bsb_war[s % 2]], inc=f"Lb{s % 2}")
                ev_bcC[s] = P.add("act", lambda e, s=s, tau=tau: e.dma_start(
                    out=csbv(s % 2),
                    in_=bcd[tau % 2].ap()[16 + s:17 + s, :].broadcast_to([128, TC])),
                    waits=[ev_bcd, csb_war[s % 2]], inc=f"Lb{2 + s % 2}")
            scan_ev = None
            last_evx = None
            yadd_last = None
            NI = S * DBM
            dA_evs = [None] * NI
            u_evs = [None] * NI

            def emit_front(j, tau=tau):
                s, db = j // DBM, j % DBM
                jj = j % 2
                dA_evs[j] = P.add("act", lambda e, db=db, jj=jj, col=db * 16 + s: e.activation(
                    dAx(jj), dtv(db), AF.Exp, scale=A_s[:, col:col + 1]),
                    waits=[dt_ready[db], dA_war[jj], ev_st0, ev_ms[jj]], inc="Ac")
                ev_u = P.add("dve", lambda e, db=db, jj=jj, s2=s % 2: e.scalar_tensor_tensor(
                    uv(jj), wv_(db), 1.0, bsbv(s2), ALU.mult, ALU.mult),
                    waits=[ev_bcB[s], u_war[jj]], inc="Dv")
                P.add("act", lambda e, jj=jj: e.activation(
                    dbxx(jj), uv(jj), AF.Relu), waits=[ev_u, dbx_war[jj]])
                u_evs[j] = P.add("act", lambda e, jj=jj, i=j: e.copy(
                    dbx0(jj), state[:, i:i + 1]), inc="Ac")

            emit_front(0)
            for i in range(NI):
                s, db = i // DBM, i % DBM
                ii = i % 2
                if i + 1 < NI:
                    emit_front(i + 1)
                evh = P.add("dve", lambda e, ii=ii: e.tensor_tensor_scan(
                    hv(ii), dAv(ii), dbxv(ii), 0.0, ALU.mult, ALU.add),
                    waits=[dA_evs[i], u_evs[i]] + h_war[ii], inc="Dv")
                scan_ev = evh
                evx = P.add("act", lambda e, ii=ii, i=i: e.copy(
                    state[:, i:i + 1], hv(ii)[:, TC:TC + 1]), waits=[evh], inc="Ac")
                last_evx = evx
                if i % 2 == 0:
                    evv = P.add("dve", lambda e, ii=ii, s2=s % 2: e.scalar_tensor_tensor(
                        vv(ii), hx(ii), 1.0, csbv(s2), ALU.mult, ALU.mult),
                        waits=[ev_bcC[s]], inc="Dv")
                else:
                    evv = P.add("pool", lambda e, ii=ii, s2=s % 2: e.tensor_mul(
                        vv(ii), hx(ii), csbv(s2)),
                        waits=[evh, ev_bcC[s]], inc="Po")
                evy = P.add("pool", lambda e, ii=ii, db=db: e.tensor_add(
                    yv(db), yv(db), vv(ii)),
                    waits=[evv] + ([y_init_ev[db]] if s == 0 else []), inc="Po")
                yadd_last = evy
                dA_war[ii] = evy
                u_war[ii] = evh
                dbx_war[ii] = evh
                h_war[ii] = [evy, evx]
                if db == DBM - 1:
                    bsb_war[s % 2] = u_evs[i]
                    csb_war[s % 2] = evv
                    if s + 2 < S:
                        sn = s + 2
                        ev_bcB[sn] = P.add("act", lambda e, sn=sn, tau=tau: e.dma_start(
                            out=bsbv(sn % 2),
                            in_=bcd[tau % 2].ap()[sn:sn + 1, :].broadcast_to([128, TC])),
                            waits=[ev_bcd, bsb_war[sn % 2]], inc=f"Lb{sn % 2}")
                        ev_bcC[sn] = P.add("act", lambda e, sn=sn, tau=tau: e.dma_start(
                            out=csbv(sn % 2),
                            in_=bcd[tau % 2].ap()[16 + sn:17 + sn, :].broadcast_to([128, TC])),
                            waits=[ev_bcd, csb_war[sn % 2]], inc=f"Lb{2 + sn % 2}")
                    puller.pull(PACE)
            prev_dtmm_last = psdt_war[1]
            bcast_last.append([ev_bcB[14], ev_bcB[15], ev_bcC[14], ev_bcC[15]])
            yb_ev = P.add("sync", lambda e, tau=tau: e.dma_start(
                out=ybin[tau].ap().rearrange("(db p) t -> p db t", p=128),
                in_=a32[:, o_y:o_y + DBM * TC].rearrange("p (db t) -> p db t", db=DBM)),
                waits=[yadd_last], inc=f"Ly{tau}")
            ag_ev = P.add("pool", lambda e, tau=tau: e.collective_compute(
                "AllGather", ALU.bypass, replica_groups=PAIRS,
                ins=[ybin[tau].ap().opt()],
                outs=[ybag.ap()[tau * D:(tau + 1) * D, :].opt()]),
                waits=[yb_ev], inc="Cc")
            ag_evs.append(ag_ev)
            chunk_bar = [scan_ev, yadd_last, yb_ev, proj_ev, ev_dtr, ev_bc, last_evx]

        while not puller.done:
            puller.pull(1e9)

        if PART == "mixer":
            prev = ag_evs[-1]
            for q in range(2 * HG):
                eq = P.add("pool", lambda e, q=q: e.dma_start(
                    out=f32v(o_yf + (q % HG) * FT, FT),
                    in_=ybag.ap()[q * 128:(q + 1) * 128, 0:FT]),
                    waits=[prev], inc="Gf")
                prev = P.add("sync", lambda e, q=q: e.dma_start(
                    out=out[(q % HG) * 128:((q % HG) + 1) * 128,
                            (q // HG) * FT:(q // HG + 1) * FT],
                    in_=f32v(o_yf + (q % HG) * FT, FT)), waits=[eq], inc="Lo0")

        P.add("sync", lambda e: e.nop(),
              waits=[(s, P.cnt[s]) for s in list(P.cnt) if is_dma_sem(s)])

        with nc.Block() as block:
            @block.sync
            def _(eng):
                P.emit_engine("sync", eng, sems)

            @block.tensor
            def _(eng):
                P.emit_engine("pe", eng, sems)

            @block.scalar
            def _(eng):
                P.emit_engine("act", eng, sems)

            @block.vector
            def _(eng):
                P.emit_engine("dve", eng, sems)

            @block.gpsimd
            def _(eng):
                P.emit_engine("pool", eng, sems)

    return nc


_NC_CACHE = {}
_LAST_RAW = None


def _get_nc():
    if "nc" not in _NC_CACHE:
        _NC_CACHE["nc"] = build_nc()
    return _NC_CACHE["nc"]


def kernel(L0_attn_x, L0_attn_conv_w, L0_attn_conv_b, L0_attn_x_proj_w,
           L0_attn_dt_proj_w, L0_attn_dt_proj_b, L0_attn_A_log, L0_attn_D_param,
           L0_ffn_w_gate, L0_ffn_w_up, L0_ffn_w_down, L0_ffn_s_gate,
           L0_ffn_s_up, L0_ffn_s_down):
    x = np.asarray(L0_attn_x, np.float32)
    conv_w = np.asarray(L0_attn_conv_w, np.float32)
    conv_b = np.asarray(L0_attn_conv_b, np.float32)
    x_proj_w = np.asarray(L0_attn_x_proj_w, np.float32)
    dt_proj_w = np.asarray(L0_attn_dt_proj_w, np.float32)
    dt_proj_b = np.asarray(L0_attn_dt_proj_b, np.float32)
    A_log = np.asarray(L0_attn_A_log, np.float32)
    D_param = np.asarray(L0_attn_D_param, np.float32)
    wg8 = np.asarray(L0_ffn_w_gate)
    wu8 = np.asarray(L0_ffn_w_up)
    wd8 = np.asarray(L0_ffn_w_down)
    s_gate = np.asarray(L0_ffn_s_gate, np.float32)
    s_up = np.asarray(L0_ffn_s_up, np.float32)
    s_down = np.asarray(L0_ffn_s_down, np.float32)

    def half_prep(role):
        lo = role * 1024
        perm = np.concatenate([np.arange(lo, lo + 1024),
                               np.arange((1 - role) * 1024, (1 - role) * 1024 + 1024)])
        d = {}
        d["cw"] = np.ascontiguousarray(
            conv_w[perm, 0, :].reshape(16, 128, 4).transpose(1, 0, 2).reshape(128, 64))
        d["cbi"] = np.ascontiguousarray(conv_b[perm].reshape(16, 128).T)
        d["dpi"] = np.ascontiguousarray(D_param[lo:lo + 1024].reshape(8, 128).T)
        d["dtbi"] = np.ascontiguousarray(dt_proj_b[lo:lo + 1024].reshape(8, 128).T)
        d["alog"] = np.ascontiguousarray(
            A_log[lo:lo + 1024].reshape(8, 128, 16).transpose(1, 0, 2).reshape(128, 128))
        d["wxp"] = np.ascontiguousarray(x_proj_w.T[perm])
        d["wdt"] = np.ascontiguousarray(dt_proj_w[lo:lo + 1024].T)
        d["offs"] = np.full((1, 4), role * FT, np.uint32)
        return d, perm

    halves = [half_prep(0), half_prep(1)]

    onescol = np.ones((1, 128), np.float32)
    ones128 = np.ones((128, 1), np.float32)
    g4 = wg8.reshape(16, 128, 64, 128).transpose(2, 1, 0, 3)
    u4 = wu8.reshape(16, 128, 64, 128).transpose(2, 1, 0, 3)
    hgu = np.ascontiguousarray(np.stack([g4, u4], axis=2).reshape(NIT, 128, 4096))
    hd = np.ascontiguousarray(wd8.reshape(NIT, 128, 2048))
    sgl = np.ascontiguousarray(s_gate.reshape(64, 128).T)
    sul = np.ascontiguousarray(s_up.reshape(64, 128).T)
    sdl = np.ascontiguousarray(s_down.reshape(16, 128).T)

    in_maps = []
    for c in range(8):
        b, role = c // 2, c % 2
        d, perm = halves[role]
        xTh = np.zeros((D, T + 3), np.float32)
        xTh[:, 3:] = x[b].T[perm]
        m = {"xT": xTh, "onescol": onescol, "ones128": ones128,
             "hgu": hgu, "hd": hd, "sgd": sgl, "sud": sul, "sdd": sdl}
        m.update(d)
        in_maps.append(m)

    nc = _get_nc()
    res = run_bass_kernel_spmd(nc, in_maps, core_ids=list(range(8)))

    global _LAST_RAW
    _LAST_RAW = [np.asarray(res.results[c]["out"]) for c in range(8)]
    result = np.empty((BATCH, T, D), np.float32)
    for c in range(8):
        b, role = c // 2, c % 2
        o = np.asarray(res.results[c]["out"])
        for j in range(NCH):
            result[b, j * TC + role * FT: j * TC + (role + 1) * FT, :] = \
                o[:, j * FT:(j + 1) * FT].T
    return result
